# version 06
# baseline (speedup 1.0000x reference)
import math
from contextlib import ExitStack

import numpy as np
import ml_dtypes

import concourse.bass as bass
import concourse.mybir as mybir
from concourse.bass_utils import run_bass_kernel_spmd

F32 = mybir.dt.float32
BF16 = mybir.dt.bfloat16
AF = mybir.ActivationFunctionType
ALU = mybir.AluOpType
AX = mybir.AxisListType

NS_COEFS = [(4.0848, -6.8946, 2.927), (3.9505, -6.3029, 2.6377),
            (3.7418, -5.5913, 2.3037), (2.8769, -3.1427, 1.2046),
            (2.8366, -3.0525, 1.2012)]
BASE_LR_INV = float(0.01 + np.log(-np.expm1(-0.01)))
RS2048 = float(1.0 / math.sqrt(2048.0))
RS8192 = float(1.0 / math.sqrt(8192.0))

ENGS = ("tensor", "vector", "scalar", "gpsimd", "sync")


class P:

    def __init__(self):
        self.items = {e: [] for e in ENGS}
        self.cnt = {e: 0 for e in ENGS}
        self.lastw = {}
        self.readers = {}
        self.watermark = {e: {} for e in ENGS}

    def _waits_for(self, eng, reads, writes):
        waits = {}

        def need(tok):
            if tok is None:
                return
            e2, c2 = tok
            if e2 == eng and e2 == "tensor":
                return
            waits[e2] = max(waits.get(e2, 0), c2)

        for r in reads:
            need(self.lastw.get(r))
        for w in writes:
            need(self.lastw.get(w))
            for t in self.readers.get(w, ()):
                need(t)
        wm = self.watermark[eng]
        out = []
        for e2, v in waits.items():
            if wm.get(e2, 0) >= v:
                continue
            wm[e2] = v
            out.append((e2, v))
        return out

    def op(self, eng, fn, reads=(), writes=(), inc=1):
        waits = self._waits_for(eng, reads, writes)
        self.cnt[eng] += inc
        me = self.cnt[eng]
        self.items[eng].append((waits, fn, inc))
        tok = (eng, me)
        for w in writes:
            self.lastw[w] = tok
            self.readers[w] = []
        for r in reads:
            self.readers.setdefault(r, []).append(tok)
        return tok

    def barrier(self):
        counts = dict(self.cnt)
        for e in ENGS:
            waits = []
            wm = self.watermark[e]
            for e2 in ENGS:
                if e2 == e or counts[e2] == 0:
                    continue
                if wm.get(e2, 0) >= counts[e2]:
                    continue
                wm[e2] = counts[e2]
                waits.append((e2, counts[e2]))
            if waits:
                self.items[e].append((waits, None, 0))
        self.readers = {}


def c128(t):
    return slice(t * 128, (t + 1) * 128)


def build_nc():
    nc = bass.Bass("TRN2", target_bir_lowering=False)
    p = P()

    rot_state = [0]

    def rot():
        rot_state[0] = (rot_state[0] + 1) % 3
        return ("scalar", "vector", "gpsimd")[rot_state[0]]

    def rot2():
        rot_state[0] = (rot_state[0] + 1) % 2
        return ("vector", "gpsimd")[rot_state[0]]

    di = lambda n, sh, dt: nc.dram_tensor(n, sh, dt, kind="ExternalInput")
    xT_d = di("xT", [2048, 4096], F32)
    xres_d = di("xres", [2048, 1024], F32)
    wqkv_d = di("wqkv", [6144, 2048], F32)
    xq_d = di("xq", [2048, 1024], F32)
    wlr_d = di("wlr", [128, 768], F32)
    wm_d = di("wm", [128, 256], F32)
    ws_d = di("ws", [128, 256], F32)
    wo_d = di("wo", [2048, 2048], F32)
    wu_d = di("wu", [8192, 2048], F32)
    wg_d = di("wg", [8192, 2048], F32)
    wd_d = di("wd", [2048, 8192], F32)
    w0n_d = di("w0n", [2048, 128], F32)
    w1n_d = di("w1n", [2048, 128], F32)
    w2n_d = di("w2n", [2048, 128], F32)
    w0t_d = di("w0t", [2048, 128], F32)
    w1t_d = di("w1t", [2048, 128], F32)
    w2t_d = di("w2t", [2048, 128], F32)
    lng_d = di("lng", [128, 1], F32)
    lnb_d = di("lnb", [1, 128], F32)
    bm_d = di("bm", [16, 1], F32)
    bs_d = di("bs", [16, 1], F32)
    suug_d = di("suug", [128, 16], F32)
    sudp_d = di("sudp", [128, 64], F32)
    wsug_d = di("wsug", [128, 128], F32)
    wsdp_d = di("wsdp", [128, 16], F32)
    idbf_d = di("idbf", [128, 128], BF16)
    h128_d = di("h128", [128, 128], BF16)
    onecol_d = di("onecol", [128, 1], BF16)
    onerow_d = di("onerow", [1, 128], BF16)
    selh_d = di("selh", [16, 2048], BF16)
    cvec_d = di("cvec", [128, 3], F32)
    sellr_d = di("sellr", [48, 48], BF16)

    outT_d = nc.dram_tensor("outT", [2048, 1024], F32, kind="ExternalOutput")

    x1pd = nc.dram_tensor("x1pd", [2048, 1024], F32)
    x1d = nc.dram_tensor("x1d", [2048, 1024], F32)
    x2d = nc.dram_tensor("x2d", [2048, 1024], F32)

    es = ExitStack()
    sems = {e: es.enter_context(nc.semaphore("sem_" + e)) for e in ENGS}

    pmA = es.enter_context(nc.psum_tensor("pmA", [128, 512], F32))
    pmB = es.enter_context(nc.psum_tensor("pmB", [128, 512], F32))
    pbc1 = es.enter_context(nc.psum_tensor("pbc1", [128, 512], F32))
    pbc2 = es.enter_context(nc.psum_tensor("pbc2", [128, 512], F32))
    ptr = es.enter_context(nc.psum_tensor("ptr", [128, 512], BF16))
    pst1 = es.enter_context(nc.psum_tensor("pst1", [1, 512], F32))
    pst2 = es.enter_context(nc.psum_tensor("pst2", [1, 512], F32))

    sbg = lambda n, sh, dt: es.enter_context(nc.sbuf_tensor(n, sh, dt))
    idbf = sbg("idbf_s", [128, 128], BF16)
    h128 = sbg("h128_s", [128, 128], BF16)
    onecol = sbg("onecol_s", [128, 1], BF16)
    onerow = sbg("onerow_s", [1, 128], BF16)
    cvec = sbg("cvec_s", [128, 3], F32)
    wf32 = [sbg("wf32_%d" % i, [128, 2048], F32) for i in range(2)]
    wbf = [sbg("wbf_%d" % i, [128, 2048], BF16) for i in range(2)]
    A = sbg("Abuf", [128, 16384], BF16)

    def dma(q, out_ap, in_ap, reads=(), writes=()):
        return p.op(q, lambda e, o=out_ap, i=in_ap: e.dma_start(out=o, in_=i),
                    reads=list(reads) + [("qser", q)],
                    writes=list(writes) + [("qser", q)], inc=16)

    def mm(out_ap, lhsT_ap, rhs_ap, reads, writes, start, stop):
        return p.op("tensor",
                    lambda e, o=out_ap, l=lhsT_ap, r=rhs_ap, s0=start, s1=stop:
                    e.matmul(o, l, r, start=s0, stop=s1),
                    reads=reads, writes=writes)

    def act(out_ap, in_ap, func, reads, writes, bias=None, scale=None):
        def f(e, o=out_ap, i=in_ap, fu=func, b=bias, s=scale):
            kw = {}
            if b is not None:
                kw["bias"] = b
            if s is not None:
                kw["scale"] = s
            return e.activation(o, i, fu, **kw)
        return p.op("scalar", f, reads=reads, writes=writes)

    def tt(eng, out_ap, a_ap, b_ap, alu, reads, writes):
        return p.op(eng, lambda e, o=out_ap, x=a_ap, y=b_ap, op=alu:
                    e.tensor_tensor(o, x, y, op), reads=reads, writes=writes)

    def tsm(eng, out_ap, in_ap, sc, reads, writes):
        return p.op(eng, lambda e, o=out_ap, i=in_ap, s=sc:
                    e.tensor_scalar_mul(o, i, s), reads=reads, writes=writes)

    def tsa(eng, out_ap, in_ap, const, reads, writes):
        return p.op(eng, lambda e, o=out_ap, i=in_ap, s=const:
                    e.tensor_scalar_add(o, i, s), reads=reads, writes=writes)

    def stt(out_ap, in0_ap, const, in1_ap, op0, op1, reads, writes):
        return p.op("vector", lambda e, o=out_ap, i0=in0_ap, s=const, i1=in1_ap,
                    a=op0, b=op1: e.scalar_tensor_tensor(o, i0, s, i1, a, b),
                    reads=reads, writes=writes)

    def cp(eng, out_ap, in_ap, reads, writes):
        if eng == "scalar":
            return act(out_ap, in_ap, AF.Copy, reads, writes)
        return p.op(eng, lambda e, o=out_ap, i=in_ap: e.tensor_copy(o, i),
                    reads=reads, writes=writes)

    def recip(ap, reads, writes):
        return p.op("vector", lambda e, a=ap: e.reciprocal(a, a),
                    reads=reads, writes=writes)

    def rsum(out_ap, in_ap, reads, writes):
        return p.op("vector", lambda e, o=out_ap, i=in_ap:
                    e.reduce_sum(o, i, axis=AX.X), reads=reads, writes=writes)

    def tr(out_ap, in_ap, reads, writes):
        return p.op("tensor", lambda e, o=out_ap, i=in_ap:
                    e.transpose(o, i, idbf[:, :]), reads=reads, writes=writes)

    dma("sync", idbf[:, :], idbf_d[:, :], writes=[("idbf",)])
    dma("sync", h128[:, :], h128_d[:, :], writes=[("h128",)])
    dma("sync", onecol[:, :], onecol_d[:, :], writes=[("onecol",)])
    dma("sync", onerow[:, :], onerow_d[:, :], writes=[("onerow",)])
    dma("sync", cvec[:, :], cvec_d[:, :], writes=[("cvec",)])

    es1 = ExitStack()
    sb1 = lambda n, sh, dt: es1.enter_context(nc.sbuf_tensor(n, sh, dt))

    qT = [sb1("qT%d" % h, [128, 1024], BF16) for h in range(4)]
    kT = [sb1("kT%d" % h, [128, 2048], BF16) for h in range(4)]
    vT = [sb1("vT%d" % h, [128, 2048], BF16) for h in range(4)]
    S_bf = sb1("S_bf", [16, 1024], BF16)
    M_bf = sb1("M_bf", [16, 2048], BF16)
    lr_bf = sb1("lr_bf", [48, 2048], BF16)
    lrc = [sb1("lrc%d" % h, [128, 48], F32) for h in range(4)]
    mom = [sb1("mom%d" % h, [128, 1], F32) for h in range(4)]
    selh = sb1("selh_s", [16, 2048], BF16)
    sellr = sb1("sellr_s", [48, 48], BF16)
    lng = sb1("lng_s", [128, 1], F32)
    lnbrow = sb1("lnbrow_s", [1, 128], F32)
    lnbrow_bf = sb1("lnbrowb_s", [1, 128], BF16)
    bm = sb1("bm_s", [16, 1], F32)
    bs = sb1("bs_s", [16, 1], F32)
    r0 = sb1("r0", [1, 512], F32)
    r1_ = sb1("r1", [1, 512], F32)
    r2_ = sb1("r2", [1, 512], F32)
    rb0 = sb1("rb0", [1, 512], BF16)
    rb1 = sb1("rb1", [1, 512], BF16)
    qsc = sb1("qsc", [128, 512], F32)
    sqb = sb1("sqb", [128, 512], BF16)
    mrow = sb1("mrow", [16, 1], F32)
    mrow_bf = sb1("mrowb", [16, 1], BF16)
    xkb = sb1("xkb", [128, 8192], BF16)
    xld = sb1("xld", [128, 512], F32)
    sc48 = [sb1("sc48_%d" % i, [48, 512], F32) for i in range(3)]

    dma("sync", selh[:, :], selh_d[:, :], writes=[("selh",)])
    dma("sync", sellr[:, :], sellr_d[:, :], writes=[("sellr",)])
    dma("sync", lng[:, :], lng_d[:, :], writes=[("lng",)])
    dma("sync", lnbrow[:, :], lnb_d[:, :], writes=[("lnbrow",)])
    dma("sync", bm[:, :], bm_d[:, :], writes=[("bm",)])
    dma("sync", bs[:, :], bs_d[:, :], writes=[("bs",)])
    cp("scalar", lnbrow_bf[:, :], lnbrow[:, :], reads=[("lnbrow",)],
       writes=[("lnbrowb",)])

    def l2norm_store(psrc, nc_, dsts):
        act(qsc[:, 0:nc_], psrc[:, 0:nc_], AF.Silu, reads=[("pmA",)],
            writes=[("qsc",)])
        act(sqb[:, 0:nc_], qsc[:, 0:nc_], AF.Square, reads=[("qsc",)],
            writes=[("sqb",)])
        mm(pst1[:, 0:nc_], onecol[:, :], sqb[:, 0:nc_],
           reads=[("onecol",), ("sqb",)],
           writes=[("pst1",)], start=True, stop=True)
        act(r0[:, 0:nc_], pst1[:, 0:nc_], AF.Sqrt, reads=[("pst1",)],
            writes=[("r0",)])
        tsa("vector", r0[:, 0:nc_], r0[:, 0:nc_], 1e-5, reads=[("r0",)],
            writes=[("r0",)])
        recip(r0[:, 0:nc_], reads=[("r0",)], writes=[("r0",)])
        cp("scalar", rb0[:, 0:nc_], r0[:, 0:nc_], reads=[("r0",)],
           writes=[("rb0",)])
        mm(pbc1[:, 0:nc_], onerow[:, :], rb0[:, 0:nc_],
           reads=[("onerow",), ("rb0",)],
           writes=[("pbc1",)], start=True, stop=True)
        for (dst_ap, dst_key, off, w) in dsts:
            tt("vector", dst_ap, qsc[:, off:off + w], pbc1[0:128, off:off + w],
               ALU.mult, reads=[("qsc",), ("pbc1",)], writes=[dst_key])

    def load_xblock(src_d, col0):
        for k in range(16):
            dma("sync", xld[:, :], src_d[c128(k), col0:col0 + 512],
                writes=[("xld",)])
            cp("scalar", xkb[:, k * 512:(k + 1) * 512], xld[:, :],
               reads=[("xld",)], writes=[("xb", k)])

    jcnt = [0]

    def proj_mm(mtile, mw, bi, nc_=512):
        jcnt[0] += 1
        wkey = ("wj", jcnt[0])
        if mw == 128:
            dma("sync", wf32[bi][:, :], wqkv_d[c128(mtile), :],
                writes=[("wf32", bi)])
            cp(rot(), wbf[bi][:, :], wf32[bi][:, :],
               reads=[("wf32", bi)], writes=[("wbf", bi), wkey])
        else:
            srcd = wlr_d if mw == 48 else (wm_d if mtile == -1 else ws_d)
            dma("sync", wf32[bi][:, 0:16 * mw], srcd[:, :],
                writes=[("wf32", bi)])
            cp(rot(), wbf[bi][:, 0:16 * mw], wf32[bi][:, 0:16 * mw],
               reads=[("wf32", bi)], writes=[("wbf", bi), wkey])
        pm = pmA
        for k in range(16):
            mm(pm[0:mw, 0:nc_] if mw < 128 else pm[:, 0:nc_],
               wbf[bi][:, k * mw:(k + 1) * mw],
               xkb[:, k * nc_:k * nc_ + nc_],
               reads=[wkey, ("xb", k)], writes=[("pmA",)],
               start=(k == 0), stop=(k == 15))
        return pm

    def fwd_block(hl, h, blk, wTs):
        tok0 = blk * 512
        rhs = qT[hl][:, tok0:tok0 + 512]
        mm(pmA[:, :], wTs[2][:, :], rhs, reads=[("wT", 2), ("qT", hl)],
           writes=[("pmA",)], start=True, stop=True)
        cp("scalar", s1[:, :], pmA[:, :], reads=[("pmA",)], writes=[("s1",)])
        mm(pmB[:, :], wTs[0][:, :], rhs, reads=[("wT", 0), ("qT", hl)],
           writes=[("pmB",)], start=True, stop=True)
        act(s2[:, :], pmB[:, :], AF.Silu, reads=[("pmB",)], writes=[("s2",)])
        tt("vector", s3[:, :], s2[:, :], s1[:, :], ALU.mult,
           reads=[("s1",), ("s2",)], writes=[("s3",)])
        mm(pmA[:, :], wTs[1][:, :], s3[:, :], reads=[("wT", 1), ("s3",)],
           writes=[("pmA",)], start=True, stop=True)
        act(s4[:, :], pmA[:, :], AF.Copy, reads=[("pmA",)], writes=[("s4",)])
        act(s5[:, :], pmA[:, :], AF.Square, reads=[("pmA",)], writes=[("s5",)])
        mm(pst1[:, :], onecol[:, :], s4[:, :], reads=[("onecol",), ("s4",)],
           writes=[("pst1",)], start=True, stop=True)
        mm(pst2[:, :], onecol[:, :], s5[:, :], reads=[("onecol",), ("s5",)],
           writes=[("pst2",)], start=True, stop=True)
        act(r0[:, :], pst1[:, :], AF.Copy, reads=[("pst1",)], writes=[("r0",)],
            scale=1.0 / 128.0)
        act(r1_[:, :], r0[:, :], AF.Square, reads=[("r0",)], writes=[("r1",)])
        stt(r2_[:, :], pst2[:, :], 1.0 / 128.0, r1_[:, :], ALU.mult, ALU.subtract,
            reads=[("pst2",), ("r1",)], writes=[("r2",)])
        act(r2_[:, :], r2_[:, :], AF.Sqrt, reads=[("r2",), ("cvec",)],
            writes=[("r2",)], bias=cvec[0:1, 1:2])
        recip(r2_[:, :], reads=[("r2",)], writes=[("r2",)])
        mm(pbc1[:, :], selh[0:16, h * 128:(h + 1) * 128],
           S_bf[0:16, tok0:tok0 + 512], reads=[("selh",), ("S_bf",)],
           writes=[("pbc1",)], start=True, stop=True)
        cp("scalar", rb1[:, :], pbc1[0:1, :], reads=[("pbc1",)], writes=[("rb1",)])
        tt("vector", r1_[:, :], r2_[:, :], pbc1[0:1, :], ALU.mult,
           reads=[("r2",), ("pbc1",)], writes=[("r1",)])
        cp("scalar", rb0[:, :], r1_[:, :], reads=[("r1",)], writes=[("rb0",)])
        mm(pbc2[:, :], onerow[:, :], rb0[:, :], reads=[("onerow",), ("rb0",)],
           writes=[("pbc2",)], start=True, stop=True)
        cp("scalar", rb0[:, :], r0[:, :], reads=[("r0",)], writes=[("rb0",)])
        mm(pbc1[:, :], onerow[:, :], rb0[:, :], reads=[("onerow",), ("rb0",)],
           writes=[("pbc1",)], start=True, stop=True)
        tt("vector", qsc[:, :], s4[:, :], pbc1[0:128, :], ALU.subtract,
           reads=[("s4",), ("pbc1",)], writes=[("qsc",)])
        tt("vector", qsc[:, :], qsc[:, :], pbc2[0:128, :], ALU.mult,
           reads=[("qsc",), ("pbc2",)], writes=[("qsc",)])
        tsm("vector", qsc[:, :], qsc[:, :], lng[:, :], reads=[("qsc",), ("lng",)],
            writes=[("qsc",)])
        mm(pbc2[:, :], lnbrow_bf[:, :], rb1[:, :], reads=[("lnbrowb",), ("rb1",)],
           writes=[("pbc2",)], start=True, stop=True)
        tt("vector", A[:, h * 1024 + tok0: h * 1024 + tok0 + 512],
           qsc[:, :], pbc2[0:128, :], ALU.add,
           reads=[("qsc",), ("pbc2",)], writes=[("A", h)])

    es_t = ExitStack()
    sbt = lambda n, sh, dt: es_t.enter_context(nc.sbuf_tensor(n, sh, dt))
    k_nat = sbt("k_nat", [128, 2048], BF16)
    v_nat = sbt("v_nat", [128, 2048], BF16)
    Hd_n = sbt("Hd_n", [128, 2048], BF16)
    DGB_n = sbt("DGB_n", [128, 2048], BF16)
    DHB_n = sbt("DHB_n", [128, 2048], BF16)
    klsc = sbt("klsc", [128, 2048], BF16)
    s1 = sbt("s1", [128, 512], BF16)
    s2 = sbt("s2", [128, 512], BF16)
    s3 = sbt("s3", [128, 512], BF16)
    s4 = sbt("s4", [128, 512], BF16)
    s5 = sbt("s5", [128, 512], BF16)
    s6 = sbt("s6", [128, 512], BF16)
    s7 = sbt("s7", [128, 512], BF16)
    s8 = sbt("s8", [128, 512], BF16)
    sH = sbt("sH", [128, 512], BF16)
    sD = sbt("sD", [128, 512], BF16)
    wN = [sbt("wN%d" % j, [128, 128], F32) for j in range(3)]
    wnorm = [sbt("wno%d" % j, [128, 1], F32) for j in range(3)]
    wT_bf = [sbt("wT%d" % j, [128, 128], BF16) for j in range(3)]
    w1nat_bf = sbt("w1natb", [128, 128], BF16)
    dwst = [sbt("dwst%d" % j, [128, 128], F32) for j in range(3)]
    X_bf = sbt("Xbf", [128, 128], BF16)
    XT_bf = sbt("XTbf", [128, 128], BF16)
    A_bf = sbt("Abf2", [128, 128], BF16)
    Bm_bf = sbt("Bmbf", [128, 128], BF16)
    nst1 = sbt("nst1", [128, 128], F32)
    nsq = sbt("nsq", [128, 128], BF16)
    wsq = sbt("wsq", [128, 128], F32)
    rA = sbt("rA", [1, 1], F32)
    rAb = sbt("rAb", [1, 1], BF16)
    nbv = sbt("nbv", [128, 1], F32)
    rh = sbt("rh", [128, 1], F32)
    wld = sbt("wld", [128, 128], F32)

    for tb in range(4):
        load_xblock(xT_d, tb * 512)
        tok0 = tb * 512
        pm = proj_mm(-2, 48, 0)
        tsa("vector", sc48[0][:, :], pm[0:48, :], BASE_LR_INV,
            reads=[("pmA",)], writes=[("sc0",)])
        act(sc48[1][:, :], sc48[0][:, :], AF.Abs, reads=[("sc0",)],
            writes=[("sc1",)])
        act(sc48[2][:, :], sc48[1][:, :], AF.Exp, reads=[("sc1",)],
            writes=[("sc2",)], scale=-1.0)
        act(sc48[1][:, :], sc48[2][:, :], AF.Ln, reads=[("sc2",)],
            writes=[("sc1",)], bias=1.0)
        act(sc48[2][:, :], sc48[0][:, :], AF.Relu, reads=[("sc0",)],
            writes=[("sc2",)])
        tt("vector", lr_bf[0:48, tok0:tok0 + 512], sc48[1][:, :],
           sc48[2][:, :], ALU.add, reads=[("sc1",), ("sc2",)],
           writes=[("lr_bf",)])
        pm = proj_mm(-1, 16, 1)
        act(M_bf[0:16, tok0:tok0 + 512], pm[0:16, :], AF.Sigmoid,
            reads=[("pmA",), ("bm",)], writes=[("M_bf",)],
            bias=bm[0:16, :])
    for tb2 in range(2):
        load_xblock(xres_d, tb2 * 512)
        tok0 = tb2 * 512
        pm = proj_mm(-3, 16, 0)
        act(S_bf[0:16, tok0:tok0 + 512], pm[0:16, :], AF.Silu,
            reads=[("pmA",), ("bs",)], writes=[("S_bf",)],
            bias=bs[0:16, :])
    rsum(mrow[:, :], M_bf[0:16, :], reads=[("M_bf",)], writes=[("mrow",)])
    p.op("vector", lambda e: e.tensor_scalar(
        mrow[:, :], mrow[:, :], 1.0 / 2048.0, 1.0, ALU.mult, ALU.add),
        reads=[("mrow",)], writes=[("mrow",)])
    cp("scalar", mrow_bf[:, :], mrow[:, :], reads=[("mrow",)],
       writes=[("mrowb",)])

    for grp in range(4):
        for k in range(16):
            for i in range(4):
                t_i = 4 * grp + i
                dma("sync", xld[:, i * 128:(i + 1) * 128],
                    xT_d[c128(k), t_i * 256: t_i * 256 + 128],
                    writes=[("xld",)])
            cp("scalar", xkb[:, k * 512:(k + 1) * 512], xld[:, :],
               reads=[("xld",)], writes=[("xb", k)])
        for h16 in range(16):
            bi = h16 % 2
            pm = proj_mm(3 * h16 + 1, 128, bi)
            l2norm_store(pm, 512,
                         [(kT[i][:, h16 * 128:(h16 + 1) * 128], ("kT", i),
                           i * 128, 128) for i in range(4)])
            pm = proj_mm(3 * h16 + 2, 128, bi ^ 1)
            for i in range(4):
                act(vT[i][:, h16 * 128:(h16 + 1) * 128],
                    pm[:, i * 128:(i + 1) * 128], AF.Silu,
                    reads=[("pmA",)], writes=[("vT", i)])
        for k in range(16):
            for half in range(2):
                dma("sync", xld[:, half * 128:(half + 1) * 128],
                    xq_d[c128(k), half * 512 + grp * 128:
                         half * 512 + grp * 128 + 128],
                    writes=[("xld",)])
            cp("scalar", xkb[:, k * 256:(k + 1) * 256], xld[:, 0:256],
               reads=[("xld",)], writes=[("xb", k)])
        for h16 in range(16):
            bi = h16 % 2
            pm = proj_mm(3 * h16, 128, bi, nc_=256)
            l2norm_store(pm, 256, [(s1[:, 0:256], ("s1",), 0, 256)])
            for i in range(4):
                for half in range(2):
                    cp("vector",
                       qT[i][:, half * 512 + h16 * 32:
                             half * 512 + h16 * 32 + 32],
                       s1[:, half * 128 + i * 32: half * 128 + i * 32 + 32],
                       reads=[("s1",)], writes=[("qT", i)])
        p.barrier()
        for hl in range(4):
            h = 4 * grp + hl
            mm(pbc1[:, 0:1], selh[0:16, h * 128:(h + 1) * 128], mrow_bf[:, :],
               reads=[("selh",), ("mrowb",)], writes=[("pbc1",)],
               start=True, stop=True)
            cp("vector", mom[hl][:, :], pbc1[:, 0:1], reads=[("pbc1",)],
               writes=[("mom", hl)])
            for t in range(16):
                mm(pbc2[:, 0:3], lr_bf[0:48, t:2048:16],
                   sellr[0:48, h * 3:h * 3 + 3],
                   reads=[("lr_bf",), ("sellr",)], writes=[("pbc2",)],
                   start=True, stop=True)
                cp("vector", lrc[hl][:, t * 3:t * 3 + 3], pbc2[:, 0:3],
                   reads=[("pbc2",)], writes=[("lrc", hl)])
        for hl in range(4):
            h = 4 * grp + hl
            for j, (nd, td) in enumerate([(w0n_d, w0t_d), (w1n_d, w1t_d),
                                          (w2n_d, w2t_d)]):
                dma("sync", wN[j][:, :], nd[c128(h), :], writes=[("wN", j)])
                dma("sync", wld[:, :], td[c128(h), :], writes=[("wld",)])
                cp("scalar", wT_bf[j][:, :], wld[:, :], reads=[("wld",)],
                   writes=[("wT", j)])
                act(wsq[:, :], wN[j][:, :], AF.Square, reads=[("wN", j)],
                    writes=[("wsq",)])
                rsum(wnorm[j][:, :], wsq[:, :], reads=[("wsq",)],
                     writes=[("wnorm", j)])
                act(wnorm[j][:, :], wnorm[j][:, :], AF.Sqrt,
                    reads=[("wnorm", j)], writes=[("wnorm", j)])
            cp("scalar", w1nat_bf[:, :], wN[1][:, :], reads=[("wN", 1)],
               writes=[("w1natb",)])

            fwd_block(hl, h, 0, wT_bf)
            p.barrier()

            for bi4 in range(4):
                t0 = bi4 * 512
                kb = kT[hl][:, t0:t0 + 512]
                mm(pmA[:, :], wT_bf[0][:, :], kb, reads=[("wT", 0), ("kT", hl)],
                   writes=[("pmA",)], start=True, stop=True)
                act(s1[:, :], pmA[:, :], AF.Sigmoid, reads=[("pmA",)],
                    writes=[("s1",)])
                act(s2[:, :], pmA[:, :], AF.Silu, reads=[("pmA",)],
                    writes=[("s2",)])
                act(s3[:, :], s1[:, :], AF.Copy, reads=[("s1",)], writes=[("s3",)],
                    scale=-1.0, bias=1.0)
                tt("vector", s4[:, :], pmA[:, :], s3[:, :], ALU.mult,
                   reads=[("pmA",), ("s3",)], writes=[("s4",)])
                act(s5[:, :], s4[:, :], AF.Copy, reads=[("s4",)], writes=[("s5",)],
                    bias=1.0)
                mm(pmB[:, :], wT_bf[2][:, :], kb, reads=[("wT", 2), ("kT", hl)],
                   writes=[("pmB",)], start=True, stop=True)
                cp("scalar", s6[:, :], pmB[:, :], reads=[("pmB",)],
                   writes=[("s6",)])
                tt("vector", sH[:, :], s2[:, :], s6[:, :], ALU.mult,
                   reads=[("s2",), ("s6",)], writes=[("sH",)])
                mm(pmA[:, :], w1nat_bf[:, :], vT[hl][:, t0:t0 + 512],
                   reads=[("w1natb",), ("vT", hl)], writes=[("pmA",)],
                   start=True, stop=True)
                cp("scalar", s7[:, :], pmA[:, :], reads=[("pmA",)],
                   writes=[("s7",)])
                tt("vector", sD[:, :], s7[:, :], s2[:, :], ALU.mult,
                   reads=[("s7",), ("s2",)], writes=[("sD",)])
                tt("vector", s8[:, :], s7[:, :], s6[:, :], ALU.mult,
                   reads=[("s7",), ("s6",)], writes=[("s8",)])
                tt("vector", s8[:, :], s8[:, :], s1[:, :], ALU.mult,
                   reads=[("s8",), ("s1",)], writes=[("s8",)])
                tt("vector", s8[:, :], s8[:, :], s5[:, :], ALU.mult,
                   reads=[("s8",), ("s5",)], writes=[("s8",)])
                for (srcten, base, skey, dstn, dkey) in [
                        (sH, 0, ("sH",), Hd_n, ("Hd_n",)),
                        (s8, 0, ("s8",), DGB_n, ("DGB_n",)),
                        (sD, 0, ("sD",), DHB_n, ("DHB_n",)),
                        (kT[hl], t0, ("kT", hl), k_nat, ("k_nat",)),
                        (vT[hl], t0, ("vT", hl), v_nat, ("v_nat",))]:
                    for i in range(4):
                        tr(ptr[:, i * 128:(i + 1) * 128],
                           srcten[:, base + i * 128: base + (i + 1) * 128],
                           reads=[skey, ("idbf",)], writes=[("ptr",)])
                    cp("vector", dstn[:, t0:t0 + 512], ptr[:, :],
                       reads=[("ptr",)], writes=[dkey])

            for j, (lhs_n, lhs_key, rsrc, rkey, lridx, jw) in enumerate([
                    (v_nat, ("v_nat",), Hd_n, ("Hd_n",), 1, 1),
                    (DGB_n, ("DGB_n",), k_nat, ("k_nat",), 0, 0),
                    (DHB_n, ("DHB_n",), k_nat, ("k_nat",), 2, 2)]):
                for t in range(16):
                    tsm(rot2(), klsc[:, c128(t)], rsrc[:, c128(t)],
                        lrc[hl][:, t * 3 + lridx:t * 3 + lridx + 1],
                        reads=[rkey, ("lrc", hl)], writes=[("klsc", t)])
                for t in range(16):
                    mm(pbc1[:, 0:128], lhs_n[:, c128(t)], klsc[:, c128(t)],
                       reads=[lhs_key, ("klsc", t)], writes=[("pbc1",)],
                       start=(t == 0), stop=(t == 15))
                tsm("vector", dwst[j][:, :], pbc1[:, 0:128], mom[hl][:, :],
                    reads=[("pbc1",), ("mom", hl)], writes=[("dw", j)])
                Xcur = dwst[j]
                xk = ("dw", j)
                act(nsq[:, :], Xcur[:, :], AF.Square, reads=[xk],
                    writes=[("nsq",)])
                mm(pst1[:, 0:128], onecol[:, :], nsq[:, :],
                   reads=[("onecol",), ("nsq",)], writes=[("pst1",)],
                   start=True, stop=True)
                rsum(rA[:, :], pst1[0:1, 0:128], reads=[("pst1",)],
                     writes=[("rA",)])
                act(rA[:, :], rA[:, :], AF.Sqrt, reads=[("rA",)], writes=[("rA",)])
                tsa("vector", rA[:, :], rA[:, :], 1e-7, reads=[("rA",)],
                    writes=[("rA",)])
                recip(rA[:, :], reads=[("rA",)], writes=[("rA",)])
                cp("scalar", rAb[:, :], rA[:, :], reads=[("rA",)],
                   writes=[("rAb",)])
                mm(pbc2[:, 0:1], onerow[:, :], rAb[:, :],
                   reads=[("onerow",), ("rAb",)], writes=[("pbc2",)],
                   start=True, stop=True)
                cp("vector", nbv[:, :], pbc2[:, 0:1], reads=[("pbc2",)],
                   writes=[("nbv",)])
                tsm("vector", Xcur[:, :], Xcur[:, :], nbv[:, :],
                    reads=[xk, ("nbv",)], writes=[xk])
                cp("scalar", X_bf[:, :], Xcur[:, :], reads=[xk],
                   writes=[("X_bf",)])
                tr(ptr[:, 0:128], X_bf[:, :], reads=[("X_bf",), ("idbf",)],
                   writes=[("ptr",)])
                cp("vector", XT_bf[:, :], ptr[:, 0:128], reads=[("ptr",)],
                   writes=[("XT_bf",)])
                for it, (ca, cb, cc_) in enumerate(NS_COEFS):
                    mm(pbc1[:, 0:128], XT_bf[:, :], XT_bf[:, :],
                       reads=[("XT_bf",)], writes=[("pbc1",)],
                       start=True, stop=True)
                    cp("scalar", A_bf[:, :], pbc1[:, 0:128], reads=[("pbc1",)],
                       writes=[("A_bf",)])
                    mm(pbc2[:, 0:128], A_bf[:, :], A_bf[:, :], reads=[("A_bf",)],
                       writes=[("pbc2",)], start=True, stop=True)
                    act(nst1[:, :], pbc2[:, 0:128], AF.Copy, reads=[("pbc2",)],
                        writes=[("nst1",)], scale=cc_)
                    stt(Bm_bf[:, :], pbc1[:, 0:128], cb, nst1[:, :], ALU.mult,
                        ALU.add, reads=[("pbc1",), ("nst1",)],
                        writes=[("Bm_bf",)])
                    mm(pbc1[:, 0:128], Bm_bf[:, :], X_bf[:, :],
                       reads=[("Bm_bf",), ("X_bf",)], writes=[("pbc1",)],
                       start=True, stop=True)
                    stt(Xcur[:, :], Xcur[:, :], ca, pbc1[:, 0:128], ALU.mult,
                        ALU.add, reads=[xk, ("pbc1",)], writes=[xk])
                    cp("scalar", X_bf[:, :], Xcur[:, :], reads=[xk],
                       writes=[("X_bf",)])
                    if it < 4:
                        tr(ptr[:, 0:128], X_bf[:, :], reads=[("X_bf",), ("idbf",)],
                           writes=[("ptr",)])
                        cp("vector", XT_bf[:, :], ptr[:, 0:128], reads=[("ptr",)],
                           writes=[("XT_bf",)])
                tt("vector", wN[jw][:, :], wN[jw][:, :], Xcur[:, :], ALU.add,
                   reads=[("wN", jw), xk], writes=[("wN", jw)])
                act(wsq[:, :], wN[jw][:, :], AF.Square, reads=[("wN", jw)],
                    writes=[("wsq",)])
                rsum(rh[:, :], wsq[:, :], reads=[("wsq",)], writes=[("rh",)])
                act(rh[:, :], rh[:, :], AF.Sqrt, reads=[("rh",)], writes=[("rh",)])
                tsa("vector", rh[:, :], rh[:, :], 1e-5, reads=[("rh",)],
                    writes=[("rh",)])
                recip(rh[:, :], reads=[("rh",)], writes=[("rh",)])
                tt("vector", rh[:, :], rh[:, :], wnorm[jw][:, :], ALU.mult,
                   reads=[("rh",), ("wnorm", jw)], writes=[("rh",)])
                tsm("vector", wN[jw][:, :], wN[jw][:, :], rh[:, :],
                    reads=[("wN", jw), ("rh",)], writes=[("wN", jw)])
                cp("scalar", X_bf[:, :], wN[jw][:, :], reads=[("wN", jw)],
                   writes=[("X_bf",)])
                tr(ptr[:, 0:128], X_bf[:, :], reads=[("X_bf",), ("idbf",)],
                   writes=[("ptr",)])
                cp("vector", wT_bf[jw][:, :], ptr[:, 0:128], reads=[("ptr",)],
                   writes=[("wT", jw)])

            p.barrier()
            fwd_block(hl, h, 1, wT_bf)

    es_t.close()
    es1.close()
    p.barrier()

    es2 = ExitStack()
    sb2 = lambda n, sh, dt: es2.enter_context(nc.sbuf_tensor(n, sh, dt))
    rs1 = sb2("rs1", [1, 1024], F32)
    rs2 = sb2("rs2", [1, 1024], F32)
    rt_ = sb2("rt", [1, 1024], F32)
    rmub = sb2("rmub", [1, 1024], BF16)
    rrsb = sb2("rrsb", [1, 1024], BF16)
    xr = [sb2("xr%d" % i, [128, 1024], F32) for i in range(2)]
    x1p = sb2("x1p", [128, 1024], F32)
    suug = sb2("suug_s", [128, 16], F32)
    sudp = sb2("sudp_s", [128, 64], F32)
    wsug = sb2("wsug_s", [128, 128], F32)
    wsdp = sb2("wsdp_s", [128, 16], F32)
    tmpb = sb2("tmpb", [128, 4096], BF16)
    tscr = sb2("tscr", [128, 512], F32)
    tscr2 = sb2("tscr2", [128, 512], F32)
    sgb = sb2("sgb", [128, 512], BF16)
    upb = sb2("upb", [128, 512], BF16)
    x2t = sb2("x2t", [128, 512], F32)
    x2b = sb2("x2b", [128, 512], BF16)
    x2q = sb2("x2q", [128, 512], BF16)

    dma("sync", suug[:, :], suug_d[:, :], writes=[("suug",)])
    dma("sync", sudp[:, :], sudp_d[:, :], writes=[("sudp",)])
    dma("sync", wsug[:, :], wsug_d[:, :], writes=[("wsug",)])
    dma("sync", wsdp[:, :], wsdp_d[:, :], writes=[("wsdp",)])

    es2a = ExitStack()
    B = es2a.enter_context(nc.sbuf_tensor("Bbuf", [128, 16384], BF16))
    C = es2a.enter_context(nc.sbuf_tensor("Cbuf", [128, 16384], BF16))
    xpb = es2a.enter_context(nc.sbuf_tensor("xpb", [128, 1024], BF16))
    xqb = es2a.enter_context(nc.sbuf_tensor("xqb", [128, 1024], BF16))
    p.op("vector", lambda e: e.memset(rs1[:, :], 0.0), writes=[("rs1",)])
    p.op("vector", lambda e: e.memset(rs2[:, :], 0.0), writes=[("rs2",)])

    for m in range(16):
        bi = m % 2
        dma("sync", wf32[bi][:, :], wo_d[c128(m), :], writes=[("wf32", bi)])
        cp(rot(), wbf[bi][:, :], wf32[bi][:, :], reads=[("wf32", bi)],
           writes=[("wbf", bi)])
        dma("gpsimd", xr[bi][:, :], xres_d[c128(m), :], writes=[("xr", bi)])
        for half in range(2):
            for k in range(16):
                mm(pmA[:, :], wbf[bi][:, c128(k)],
                   A[:, k * 1024 + half * 512: k * 1024 + half * 512 + 512],
                   reads=[("wbf", bi), ("A", k)], writes=[("pmA",)],
                   start=(k == 0), stop=(k == 15))
            tt("vector", x1p[:, half * 512:(half + 1) * 512], pmA[:, :],
               xr[bi][:, half * 512:(half + 1) * 512], ALU.add,
               reads=[("pmA",), ("xr", bi)], writes=[("x1p",)])
        cp("scalar", xpb[:, :], x1p[:, :], reads=[("x1p",)], writes=[("xpb",)])
        act(xqb[:, :], x1p[:, :], AF.Square, reads=[("x1p",)], writes=[("xqb",)])
        dma("gpsimd", x1pd[c128(m), :], x1p[:, :], reads=[("x1p",)],
            writes=[("x1pd", m)])
        for half in range(2):
            hs = slice(half * 512, (half + 1) * 512)
            mm(pst1[:, :], onecol[:, :], xpb[:, hs], reads=[("onecol",), ("xpb",)],
               writes=[("pst1",)], start=True, stop=True)
            tt("vector", rs1[:, hs], rs1[:, hs], pst1[:, :], ALU.add,
               reads=[("rs1",), ("pst1",)], writes=[("rs1",)])
            mm(pst2[:, :], onecol[:, :], xqb[:, hs], reads=[("onecol",), ("xqb",)],
               writes=[("pst2",)], start=True, stop=True)
            tt("vector", rs2[:, hs], rs2[:, hs], pst2[:, :], ALU.add,
               reads=[("rs2",), ("pst2",)], writes=[("rs2",)])

    def ln_finalize(eps):
        tsm("vector", rs1[:, :], rs1[:, :], 1.0 / 2048.0,
            reads=[("rs1",)], writes=[("rs1",)])
        act(rt_[:, :], rs1[:, :], AF.Square, reads=[("rs1",)], writes=[("rt",)])
        stt(rs2[:, :], rs2[:, :], 1.0 / 2048.0, rt_[:, :], ALU.mult, ALU.subtract,
            reads=[("rs2",), ("rt",)], writes=[("rs2",)])
        act(rs2[:, :], rs2[:, :], AF.Sqrt, reads=[("rs2",), ("cvec",)],
            writes=[("rs2",)], bias=cvec[0:1, 2:3])
        recip(rs2[:, :], reads=[("rs2",)], writes=[("rs2",)])
        cp("scalar", rmub[:, :], rs1[:, :], reads=[("rs1",)], writes=[("rmub",)])
        cp("scalar", rrsb[:, :], rs2[:, :], reads=[("rs2",)], writes=[("rrsb",)])

    ln_finalize(1e-5)

    for m in range(16):
        bi = m % 2
        dma("sync", xr[bi][:, :], x1pd[c128(m), :], reads=[("x1pd", m)],
            writes=[("xr", bi)])
        for half in range(2):
            hs = slice(half * 512, (half + 1) * 512)
            mm(pbc1[:, :], onerow[:, :], rmub[:, hs], reads=[("onerow",), ("rmub",)],
               writes=[("pbc1",)], start=True, stop=True)
            mm(pbc2[:, :], onerow[:, :], rrsb[:, hs], reads=[("onerow",), ("rrsb",)],
               writes=[("pbc2",)], start=True, stop=True)
            tt("vector", x1p[:, hs], xr[bi][:, hs], pbc1[0:128, :], ALU.subtract,
               reads=[("xr", bi), ("pbc1",)], writes=[("x1p",)])
            tt("vector", x1p[:, hs], x1p[:, hs], pbc2[0:128, :], ALU.mult,
               reads=[("x1p",), ("pbc2",)], writes=[("x1p",)])
        cp("scalar", B[:, m * 1024:(m + 1) * 1024], x1p[:, :],
           reads=[("x1p",)], writes=[("B", m)])
        dma("gpsimd", x1d[c128(m), :], x1p[:, :], reads=[("x1p",)],
            writes=[("x1d", m)])

    for t in range(16):
        tsm(rot2(), C[:, t * 1024:(t + 1) * 1024], B[:, t * 1024:(t + 1) * 1024],
            suug[:, t:t + 1], reads=[("B", t), ("suug",)], writes=[("C", t)])

    def butterfly(buf, bufkey, ntiles, width, stage_d):
        for pr in range(ntiles // (2 * stage_d)):
            a0 = (2 * pr) * stage_d
            b0 = (2 * pr + 1) * stage_d
            blockw = stage_d * width
            off = 0
            while off < blockw:
                w = min(4096, blockw - off)
                aslc = slice(a0 * width + off, a0 * width + off + w)
                bslc = slice(b0 * width + off, b0 * width + off + w)
                akeys = [(bufkey, a0 + (off + q) // width)
                         for q in range(0, w, width)]
                bkeys = [(bufkey, b0 + (off + q) // width)
                         for q in range(0, w, width)]
                cp("gpsimd", tmpb[:, 0:w], buf[:, aslc],
                   reads=akeys, writes=[("tmpb",)])
                tt("vector", buf[:, aslc], tmpb[:, 0:w], buf[:, bslc], ALU.add,
                   reads=[("tmpb",)] + bkeys, writes=akeys)
                tt("gpsimd", buf[:, bslc], tmpb[:, 0:w], buf[:, bslc],
                   ALU.subtract, reads=[("tmpb",)] + bkeys, writes=bkeys)
                off += w

    for d in (1, 2, 4, 8):
        butterfly(C, "C", 16, 1024, d)

    p.barrier()
    for t in range(16):
        for half in range(2):
            hs = slice(t * 1024 + half * 512, t * 1024 + half * 512 + 512)
            mm(pbc1[:, :], h128[:, :], C[:, hs], reads=[("h128",), ("C", t)],
               writes=[("pbc1",)], start=True, stop=True)
            cp("scalar" if half == 0 else "vector", A[:, hs], pbc1[:, :],
               reads=[("pbc1",)], writes=[("A2", t)])

    es2a.close()
    p.barrier()

    es2b = ExitStack()
    g = es2b.enter_context(nc.sbuf_tensor("gbuf", [128, 32768], BF16))
    wdb = es2b.enter_context(nc.sbuf_tensor("wdb", [128, 8192], BF16))

    p.op("vector", lambda e: e.memset(rs1[:, :], 0.0), writes=[("rs1",)])
    p.op("vector", lambda e: e.memset(rs2[:, :], 0.0), writes=[("rs2",)])

    for th in range(2):
        hoff = th * 512
        for m in range(64):
            biu = m % 2
            big = (m + 1) % 2
            dma("sync", wf32[biu][:, :], wu_d[c128(m), :],
                writes=[("wf32", biu)])
            cp(rot(), wbf[biu][:, :], wf32[biu][:, :], reads=[("wf32", biu)],
               writes=[("wbf", biu)])
            for k in range(16):
                mm(pmA[:, :], wbf[biu][:, c128(k)],
                   A[:, k * 1024 + hoff: k * 1024 + hoff + 512],
                   reads=[("wbf", biu), ("A2", k)], writes=[("pmA",)],
                   start=(k == 0), stop=(k == 15))
            tsm("vector", tscr[:, :], pmA[:, :], wsug[:, m:m + 1],
                reads=[("pmA",), ("wsug",)], writes=[("tscr",)])
            act(upb[:, :], tscr[:, :], AF.Copy, reads=[("tscr",)],
                writes=[("upb",)], scale=RS2048)
            dma("scalar", wf32[big][:, :], wg_d[c128(m), :],
                writes=[("wf32", big)])
            cp(rot(), wbf[big][:, :], wf32[big][:, :], reads=[("wf32", big)],
               writes=[("wbf", big)])
            for k in range(16):
                mm(pmB[:, :], wbf[big][:, c128(k)],
                   A[:, k * 1024 + hoff: k * 1024 + hoff + 512],
                   reads=[("wbf", big), ("A2", k)], writes=[("pmB",)],
                   start=(k == 0), stop=(k == 15))
            tsm("vector", tscr2[:, :], pmB[:, :], wsug[:, 64 + m:65 + m],
                reads=[("pmB",), ("wsug",)], writes=[("tscr2",)])
            act(sgb[:, :], tscr2[:, :], AF.Silu, reads=[("tscr2",)],
                writes=[("sgb",)], scale=RS2048)
            tt("vector", tscr[:, :], sgb[:, :], upb[:, :], ALU.mult,
               reads=[("sgb",), ("upb",)], writes=[("tscr",)])
            tsm("gpsimd", g[:, m * 512:(m + 1) * 512], tscr[:, :],
                sudp[:, m:m + 1], reads=[("tscr",), ("sudp",)],
                writes=[("G", m)])

        for d in (1, 2, 4, 8, 16, 32):
            butterfly(g, "G", 64, 512, d)
        for t in range(64):
            gs = slice(t * 512, (t + 1) * 512)
            mm(pbc1[:, :], h128[:, :], g[:, gs], reads=[("h128",), ("G", t)],
               writes=[("pbc1",)], start=True, stop=True)
            cp("scalar" if t % 2 == 0 else "vector", g[:, gs], pbc1[:, :],
               reads=[("pbc1",)], writes=[("G", t)])

        for m in range(16):
            bi = m % 2
            for ch in range(4):
                dma("sync", wf32[bi][:, :],
                    wd_d[c128(m), ch * 2048:(ch + 1) * 2048],
                    writes=[("wf32", bi)])
                cp(rot(), wdb[:, ch * 2048:(ch + 1) * 2048], wf32[bi][:, :],
                   reads=[("wf32", bi)], writes=[("wdb", ch)])
                for kk in range(16):
                    k = ch * 16 + kk
                    mm(pmA[:, :], wdb[:, c128(k)], g[:, k * 512:(k + 1) * 512],
                       reads=[("wdb", ch), ("G", k)], writes=[("pmA",)],
                       start=(k == 0), stop=(k == 63))
            tsm("vector", tscr[:, :], pmA[:, :], wsdp[:, m:m + 1],
                reads=[("pmA",), ("wsdp",)], writes=[("tscr",)])
            dma("gpsimd", xr[0][:, :], x1d[c128(m), :], reads=[("x1d", m)],
                writes=[("xr", 0)])
            stt(x2t[:, :], tscr[:, :], RS8192, xr[0][:, hoff:hoff + 512],
                ALU.mult, ALU.add, reads=[("tscr",), ("xr", 0)],
                writes=[("x2t",)])
            cp("scalar", x2b[:, :], x2t[:, :], reads=[("x2t",)], writes=[("x2b",)])
            act(x2q[:, :], x2t[:, :], AF.Square, reads=[("x2t",)],
                writes=[("x2q",)])
            dma("gpsimd", x2d[c128(m), hoff:hoff + 512], x2t[:, :],
                reads=[("x2t",)], writes=[("x2d", m)])
            hs = slice(hoff, hoff + 512)
            mm(pst1[:, :], onecol[:, :], x2b[:, :], reads=[("onecol",), ("x2b",)],
               writes=[("pst1",)], start=True, stop=True)
            tt("vector", rs1[:, hs], rs1[:, hs], pst1[:, :], ALU.add,
               reads=[("rs1",), ("pst1",)], writes=[("rs1",)])
            mm(pst2[:, :], onecol[:, :], x2q[:, :], reads=[("onecol",), ("x2q",)],
               writes=[("pst2",)], start=True, stop=True)
            tt("vector", rs2[:, hs], rs2[:, hs], pst2[:, :], ALU.add,
               reads=[("rs2",), ("pst2",)], writes=[("rs2",)])

    ln_finalize(1e-5)
    for m in range(16):
        bi = m % 2
        dma("sync", xr[bi][:, :], x2d[c128(m), :], reads=[("x2d", m)],
            writes=[("xr", bi)])
        for half in range(2):
            hs = slice(half * 512, (half + 1) * 512)
            mm(pbc1[:, :], onerow[:, :], rmub[:, hs], reads=[("onerow",), ("rmub",)],
               writes=[("pbc1",)], start=True, stop=True)
            mm(pbc2[:, :], onerow[:, :], rrsb[:, hs], reads=[("onerow",), ("rrsb",)],
               writes=[("pbc2",)], start=True, stop=True)
            tt("vector", x1p[:, hs], xr[bi][:, hs], pbc1[0:128, :], ALU.subtract,
               reads=[("xr", bi), ("pbc1",)], writes=[("x1p",)])
            tt("vector", x1p[:, hs], x1p[:, hs], pbc2[0:128, :], ALU.mult,
               reads=[("x1p",), ("pbc2",)], writes=[("x1p",)])
        dma("gpsimd", outT_d[c128(m), :], x1p[:, :], reads=[("x1p",)],
            writes=[("outd", m)])

    p.barrier()
    es2b.close()
    es2.close()

    with nc.Block() as block:
        def mk(engname):
            def section(eng, _items=p.items[engname], _n=engname):
                for waits, fn, inc in _items:
                    for (src, val) in waits:
                        eng.wait_ge(sems[src], val)
                    if fn is not None:
                        instr = fn(eng)
                        if inc:
                            instr.then_inc(sems[_n], inc)
            return section

        block.tensor(mk("tensor"))
        block.vector(mk("vector"))
        block.scalar(mk("scalar"))
        block.gpsimd(mk("gpsimd"))
        block.sync(mk("sync"))

    es.close()
    return nc


def _blk(w):
    K, M = w.shape
    return np.ascontiguousarray(
        w.reshape(K // 128, 128, M // 128, 128).transpose(2, 1, 0, 3)
        .reshape(M, K))


def _hadamard128():
    H = np.array([[1.0]], dtype=np.float64)
    for _ in range(7):
        H = np.block([[H, H], [H, -H]])
    return H.astype(ml_dtypes.bfloat16)


def kernel(**inp):
    bf = ml_dtypes.bfloat16
    f32 = np.float32
    x = np.asarray(inp["L0_attn_x"], f32)
    Wqkv = np.asarray(inp["L0_attn_W_qkv"], f32)
    Wlr = np.asarray(inp["L0_attn_W_lr"], f32)
    Wo = np.asarray(inp["L0_attn_W_o"], f32)
    Wm = np.asarray(inp["L0_attn_W_m"], f32)
    b_m = np.asarray(inp["L0_attn_b_m"], f32)
    Ws = np.asarray(inp["L0_attn_W_s"], f32)
    b_s = np.asarray(inp["L0_attn_b_s"], f32)
    ln_g = np.asarray(inp["L0_attn_ln_g"], f32)
    ln_b = np.asarray(inp["L0_attn_ln_b"], f32)
    w0 = np.asarray(inp["L0_attn_w0"], f32)
    w1 = np.asarray(inp["L0_attn_w1"], f32)
    w2 = np.asarray(inp["L0_attn_w2"], f32)
    SU_ug = np.asarray(inp["L0_ffn_SU_ug"], f32)
    SU_dp = np.asarray(inp["L0_ffn_SU_dp"], f32)
    Wsc_ug = np.asarray(inp["L0_ffn_Wscale_ug"], f32)
    Wsc_dp = np.asarray(inp["L0_ffn_Wscale_dp"], f32)
    W_up = np.asarray(inp["L0_ffn_W_up"], f32)
    W_gate = np.asarray(inp["L0_ffn_W_gate"], f32)
    W_down = np.asarray(inp["L0_ffn_W_down"], f32)

    nc = build_nc()

    h128 = np.ascontiguousarray(_hadamard128())
    idbf = np.eye(128, dtype=bf)
    onecol = np.ones((128, 1), dtype=bf)
    onerow = np.ones((1, 128), dtype=bf)

    xT = [np.ascontiguousarray(x[b].T) for b in range(2)]
    wu_b = _blk(np.ascontiguousarray(W_up.T))
    wg_b = _blk(np.ascontiguousarray(W_gate.T))
    wd_b = _blk(np.ascontiguousarray(W_down.T))
    wlr_b = np.ascontiguousarray(
        Wlr.reshape(16, 128, 48).transpose(1, 0, 2).reshape(128, 768))
    wm_b = np.ascontiguousarray(
        Wm.reshape(16, 128, 16).transpose(1, 0, 2).reshape(128, 256))
    ws_b = np.ascontiguousarray(
        Ws.reshape(16, 128, 16).transpose(1, 0, 2).reshape(128, 256))

    in_maps = []
    wqkv_b = _blk(Wqkv)
    wo_full = _blk(Wo)
    selh = np.zeros((16, 2048), dtype=bf)
    for h in range(16):
        selh[h, h * 128:(h + 1) * 128] = 1
    sellr = np.zeros((48, 48), dtype=bf)
    for h in range(16):
        for j in range(3):
            sellr[j * 16 + h, h * 3 + j] = 1
    w_com = {
        "wqkv": wqkv_b, "wlr": wlr_b, "wm": wm_b, "ws": ws_b,
        "wo": wo_full, "wu": wu_b, "wg": wg_b, "wd": wd_b,
        "w0n": np.ascontiguousarray(w0.reshape(2048, 128)),
        "w1n": np.ascontiguousarray(w1.reshape(2048, 128)),
        "w2n": np.ascontiguousarray(w2.reshape(2048, 128)),
        "w0t": np.ascontiguousarray(w0.transpose(0, 2, 1).reshape(2048, 128)),
        "w1t": np.ascontiguousarray(w1.transpose(0, 2, 1).reshape(2048, 128)),
        "w2t": np.ascontiguousarray(w2.transpose(0, 2, 1).reshape(2048, 128)),
        "lng": np.ascontiguousarray(ln_g.reshape(128, 1)),
        "lnb": np.ascontiguousarray(ln_b.reshape(1, 128)),
        "bm": np.ascontiguousarray(b_m.reshape(16, 1)),
        "bs": np.ascontiguousarray(b_s.reshape(16, 1)),
        "suug": np.ascontiguousarray(SU_ug.reshape(16, 128).T),
        "sudp": np.ascontiguousarray(SU_dp.reshape(64, 128).T),
        "wsug": np.ascontiguousarray(Wsc_ug.reshape(128, 128).T),
        "wsdp": np.ascontiguousarray(Wsc_dp.reshape(16, 128).T),
        "idbf": idbf, "h128": h128, "onecol": onecol, "onerow": onerow,
        "cvec": np.tile(np.array([[BASE_LR_INV, 1e-6, 1e-5]], np.float32),
                        (128, 1)),
        "selh": selh, "sellr": sellr,
    }
    off = np.arange(32)
    for c in range(8):
        beta, gidx = c // 4, c % 4
        iq = np.concatenate([t * 256 + 128 * half + 32 * gidx + off
                             for half in (0, 1) for t in range(16)])
        ires = np.concatenate([2048 * half + 512 * gidx + off * 16 + h
                               for half in (0, 1) for h in range(16)])
        m = dict(w_com)
        m["xT"] = xT[beta]
        m["xq"] = np.ascontiguousarray(xT[beta][:, iq])
        m["xres"] = np.ascontiguousarray(xT[beta][:, ires])
        m["_ires"] = ires
        in_maps.append(m)

    dev_maps = [{k: v for k, v in m.items() if not k.startswith("_")}
                for m in in_maps]
    res = run_bass_kernel_spmd(nc, dev_maps, core_ids=list(range(8)))

    out = np.empty((2, 4096, 2048), np.float32)
    for c in range(8):
        beta = c // 4
        oT = np.asarray(res.results[c]["outT"])
        out[beta, in_maps[c]["_ires"], :] = oT.T
    return out


if __name__ == "__main__":
    nc = build_nc()
    print("build OK")


# version 07 (final)
# speedup vs baseline: 1.0465x; 1.0465x over previous
import math
from contextlib import ExitStack

import numpy as np
import ml_dtypes

import concourse.bass as bass
import concourse.mybir as mybir
from concourse.bass_utils import run_bass_kernel_spmd

F32 = mybir.dt.float32
BF16 = mybir.dt.bfloat16
AF = mybir.ActivationFunctionType
ALU = mybir.AluOpType
AX = mybir.AxisListType

NS_COEFS = [(4.0848, -6.8946, 2.927), (3.9505, -6.3029, 2.6377),
            (3.7418, -5.5913, 2.3037), (2.8769, -3.1427, 1.2046),
            (2.8366, -3.0525, 1.2012)]
BASE_LR_INV = float(0.01 + np.log(-np.expm1(-0.01)))
RS2048 = float(1.0 / math.sqrt(2048.0))
RS8192 = float(1.0 / math.sqrt(8192.0))

ENGS = ("tensor", "vector", "scalar", "gpsimd", "sync")


class P:

    def __init__(self):
        self.items = {e: [] for e in ENGS}
        self.cnt = {e: 0 for e in ENGS}
        self.lastw = {}
        self.readers = {}
        self.watermark = {e: {} for e in ENGS}

    def _waits_for(self, eng, reads, writes):
        waits = {}

        def need(tok):
            if tok is None:
                return
            e2, c2 = tok
            if e2 == eng and e2 == "tensor":
                return
            waits[e2] = max(waits.get(e2, 0), c2)

        for r in reads:
            need(self.lastw.get(r))
        for w in writes:
            need(self.lastw.get(w))
            for t in self.readers.get(w, ()):
                need(t)
        wm = self.watermark[eng]
        out = []
        for e2, v in waits.items():
            if wm.get(e2, 0) >= v:
                continue
            wm[e2] = v
            out.append((e2, v))
        return out

    def op(self, eng, fn, reads=(), writes=(), inc=1):
        waits = self._waits_for(eng, reads, writes)
        self.cnt[eng] += inc
        me = self.cnt[eng]
        self.items[eng].append((waits, fn, inc))
        tok = (eng, me)
        for w in writes:
            self.lastw[w] = tok
            self.readers[w] = []
        for r in reads:
            self.readers.setdefault(r, []).append(tok)
        return tok

    def barrier(self):
        counts = dict(self.cnt)
        for e in ENGS:
            waits = []
            wm = self.watermark[e]
            for e2 in ENGS:
                if e2 == e or counts[e2] == 0:
                    continue
                if wm.get(e2, 0) >= counts[e2]:
                    continue
                wm[e2] = counts[e2]
                waits.append((e2, counts[e2]))
            if waits:
                self.items[e].append((waits, None, 0))
        self.readers = {}


def c128(t):
    return slice(t * 128, (t + 1) * 128)


def build_nc():
    nc = bass.Bass("TRN2", target_bir_lowering=False)
    p = P()

    rot_state = [0]

    def rot():
        rot_state[0] = (rot_state[0] + 1) % 3
        return ("scalar", "vector", "gpsimd")[rot_state[0]]

    def rot2():
        rot_state[0] = (rot_state[0] + 1) % 2
        return ("vector", "gpsimd")[rot_state[0]]

    di = lambda n, sh, dt: nc.dram_tensor(n, sh, dt, kind="ExternalInput")
    xT_d = di("xT", [2048, 4096], F32)
    xres_d = di("xres", [2048, 1024], F32)
    wqkv_d = di("wqkv", [6144, 2048], F32)
    xq_d = di("xq", [2048, 1024], F32)
    wlr_d = di("wlr", [128, 768], F32)
    wm_d = di("wm", [128, 256], F32)
    ws_d = di("ws", [128, 256], F32)
    wo_d = di("wo", [2048, 2048], F32)
    wu_d = di("wu", [8192, 2048], F32)
    wg_d = di("wg", [8192, 2048], F32)
    wd_d = di("wd", [2048, 8192], F32)
    w0n_d = di("w0n", [2048, 128], F32)
    w1n_d = di("w1n", [2048, 128], F32)
    w2n_d = di("w2n", [2048, 128], F32)
    w0t_d = di("w0t", [2048, 128], F32)
    w1t_d = di("w1t", [2048, 128], F32)
    w2t_d = di("w2t", [2048, 128], F32)
    lng_d = di("lng", [128, 1], F32)
    lnb_d = di("lnb", [1, 128], F32)
    bm_d = di("bm", [16, 1], F32)
    bs_d = di("bs", [16, 1], F32)
    suug_d = di("suug", [128, 16], F32)
    sudp_d = di("sudp", [128, 64], F32)
    wsug_d = di("wsug", [128, 128], F32)
    wsdp_d = di("wsdp", [128, 16], F32)
    idbf_d = di("idbf", [128, 128], BF16)
    h128_d = di("h128", [128, 128], BF16)
    onecol_d = di("onecol", [128, 1], BF16)
    onerow_d = di("onerow", [1, 128], BF16)
    selh_d = di("selh", [16, 2048], BF16)
    cvec_d = di("cvec", [128, 3], F32)
    sellr_d = di("sellr", [48, 48], BF16)

    outT_d = nc.dram_tensor("outT", [2048, 1024], F32, kind="ExternalOutput")

    x1pd = nc.dram_tensor("x1pd", [2048, 1024], F32)
    x1d = nc.dram_tensor("x1d", [2048, 1024], F32)
    x2d = nc.dram_tensor("x2d", [2048, 1024], F32)

    es = ExitStack()
    sems = {e: es.enter_context(nc.semaphore("sem_" + e)) for e in ENGS}

    pmA = es.enter_context(nc.psum_tensor("pmA", [128, 512], F32))
    pmB = es.enter_context(nc.psum_tensor("pmB", [128, 512], F32))
    pbc1 = es.enter_context(nc.psum_tensor("pbc1", [128, 512], F32))
    pbc2 = es.enter_context(nc.psum_tensor("pbc2", [128, 512], F32))
    ptr = es.enter_context(nc.psum_tensor("ptr", [128, 512], BF16))
    pst1 = es.enter_context(nc.psum_tensor("pst1", [1, 512], F32))
    pst2 = es.enter_context(nc.psum_tensor("pst2", [1, 512], F32))

    sbg = lambda n, sh, dt: es.enter_context(nc.sbuf_tensor(n, sh, dt))
    idbf = sbg("idbf_s", [128, 128], BF16)
    h128 = sbg("h128_s", [128, 128], BF16)
    onecol = sbg("onecol_s", [128, 1], BF16)
    onerow = sbg("onerow_s", [1, 128], BF16)
    cvec = sbg("cvec_s", [128, 3], F32)
    wf32 = [sbg("wf32_%d" % i, [128, 2048], F32) for i in range(2)]
    wbf = [sbg("wbf_%d" % i, [128, 2048], BF16) for i in range(2)]
    A = sbg("Abuf", [128, 16384], BF16)

    def dma(q, out_ap, in_ap, reads=(), writes=()):
        return p.op(q, lambda e, o=out_ap, i=in_ap: e.dma_start(out=o, in_=i),
                    reads=list(reads) + [("qser", q)],
                    writes=list(writes) + [("qser", q)], inc=16)

    def mm(out_ap, lhsT_ap, rhs_ap, reads, writes, start, stop):
        return p.op("tensor",
                    lambda e, o=out_ap, l=lhsT_ap, r=rhs_ap, s0=start, s1=stop:
                    e.matmul(o, l, r, start=s0, stop=s1),
                    reads=reads, writes=writes)

    def act(out_ap, in_ap, func, reads, writes, bias=None, scale=None):
        def f(e, o=out_ap, i=in_ap, fu=func, b=bias, s=scale):
            kw = {}
            if b is not None:
                kw["bias"] = b
            if s is not None:
                kw["scale"] = s
            return e.activation(o, i, fu, **kw)
        return p.op("scalar", f, reads=reads, writes=writes)

    def tt(eng, out_ap, a_ap, b_ap, alu, reads, writes):
        return p.op(eng, lambda e, o=out_ap, x=a_ap, y=b_ap, op=alu:
                    e.tensor_tensor(o, x, y, op), reads=reads, writes=writes)

    def tsm(eng, out_ap, in_ap, sc, reads, writes):
        return p.op(eng, lambda e, o=out_ap, i=in_ap, s=sc:
                    e.tensor_scalar_mul(o, i, s), reads=reads, writes=writes)

    def tsa(eng, out_ap, in_ap, const, reads, writes):
        return p.op(eng, lambda e, o=out_ap, i=in_ap, s=const:
                    e.tensor_scalar_add(o, i, s), reads=reads, writes=writes)

    def stt(out_ap, in0_ap, const, in1_ap, op0, op1, reads, writes):
        return p.op("vector", lambda e, o=out_ap, i0=in0_ap, s=const, i1=in1_ap,
                    a=op0, b=op1: e.scalar_tensor_tensor(o, i0, s, i1, a, b),
                    reads=reads, writes=writes)

    def cp(eng, out_ap, in_ap, reads, writes):
        if eng == "scalar":
            return act(out_ap, in_ap, AF.Copy, reads, writes)
        return p.op(eng, lambda e, o=out_ap, i=in_ap: e.tensor_copy(o, i),
                    reads=reads, writes=writes)

    def recip(ap, reads, writes):
        return p.op("vector", lambda e, a=ap: e.reciprocal(a, a),
                    reads=reads, writes=writes)

    def rsum(out_ap, in_ap, reads, writes):
        return p.op("vector", lambda e, o=out_ap, i=in_ap:
                    e.reduce_sum(o, i, axis=AX.X), reads=reads, writes=writes)

    def tr(out_ap, in_ap, reads, writes):
        return p.op("tensor", lambda e, o=out_ap, i=in_ap:
                    e.transpose(o, i, idbf[:, :]), reads=reads, writes=writes)

    dma("sync", idbf[:, :], idbf_d[:, :], writes=[("idbf",)])
    dma("sync", h128[:, :], h128_d[:, :], writes=[("h128",)])
    dma("sync", onecol[:, :], onecol_d[:, :], writes=[("onecol",)])
    dma("sync", onerow[:, :], onerow_d[:, :], writes=[("onerow",)])
    dma("sync", cvec[:, :], cvec_d[:, :], writes=[("cvec",)])

    es1 = ExitStack()
    sb1 = lambda n, sh, dt: es1.enter_context(nc.sbuf_tensor(n, sh, dt))

    qT = [sb1("qT%d" % h, [128, 1024], BF16) for h in range(4)]
    kT = [sb1("kT%d" % h, [128, 2048], BF16) for h in range(4)]
    vT = [sb1("vT%d" % h, [128, 2048], BF16) for h in range(4)]
    S_bf = sb1("S_bf", [16, 1024], BF16)
    M_bf = sb1("M_bf", [16, 2048], BF16)
    lr_bf = sb1("lr_bf", [48, 2048], BF16)
    lrc = [sb1("lrc%d" % h, [128, 48], F32) for h in range(4)]
    mom = [sb1("mom%d" % h, [128, 1], F32) for h in range(4)]
    selh = sb1("selh_s", [16, 2048], BF16)
    sellr = sb1("sellr_s", [48, 48], BF16)
    lng = sb1("lng_s", [128, 1], F32)
    lnbrow = sb1("lnbrow_s", [1, 128], F32)
    lnbrow_bf = sb1("lnbrowb_s", [1, 128], BF16)
    bm = sb1("bm_s", [16, 1], F32)
    bs = sb1("bs_s", [16, 1], F32)
    r0 = sb1("r0", [1, 512], F32)
    r1_ = sb1("r1", [1, 512], F32)
    r2_ = sb1("r2", [1, 512], F32)
    rb0 = sb1("rb0", [1, 512], BF16)
    rb1 = sb1("rb1", [1, 512], BF16)
    qsc = sb1("qsc", [128, 512], F32)
    sqb = sb1("sqb", [128, 512], BF16)
    mrow = sb1("mrow", [16, 1], F32)
    mrow_bf = sb1("mrowb", [16, 1], BF16)
    xkb = sb1("xkb", [128, 8192], BF16)
    xld2_ = [sb1("xld%d" % i, [128, 512], F32) for i in range(2)]
    xld = xld2_[0]
    sc48 = [sb1("sc48_%d" % i, [48, 512], F32) for i in range(3)]

    dma("sync", selh[:, :], selh_d[:, :], writes=[("selh",)])
    dma("sync", sellr[:, :], sellr_d[:, :], writes=[("sellr",)])
    dma("sync", lng[:, :], lng_d[:, :], writes=[("lng",)])
    dma("sync", lnbrow[:, :], lnb_d[:, :], writes=[("lnbrow",)])
    dma("sync", bm[:, :], bm_d[:, :], writes=[("bm",)])
    dma("sync", bs[:, :], bs_d[:, :], writes=[("bs",)])
    cp("scalar", lnbrow_bf[:, :], lnbrow[:, :], reads=[("lnbrow",)],
       writes=[("lnbrowb",)])

    def l2norm_store(psrc, nc_, dsts):
        act(qsc[:, 0:nc_], psrc[:, 0:nc_], AF.Silu, reads=[("pmA",)],
            writes=[("qsc",)])
        act(sqb[:, 0:nc_], qsc[:, 0:nc_], AF.Square, reads=[("qsc",)],
            writes=[("sqb",)])
        mm(pst1[:, 0:nc_], onecol[:, :], sqb[:, 0:nc_],
           reads=[("onecol",), ("sqb",)],
           writes=[("pst1",)], start=True, stop=True)
        act(r0[:, 0:nc_], pst1[:, 0:nc_], AF.Sqrt, reads=[("pst1",)],
            writes=[("r0",)])
        tsa("vector", r0[:, 0:nc_], r0[:, 0:nc_], 1e-5, reads=[("r0",)],
            writes=[("r0",)])
        recip(r0[:, 0:nc_], reads=[("r0",)], writes=[("r0",)])
        cp("scalar", rb0[:, 0:nc_], r0[:, 0:nc_], reads=[("r0",)],
           writes=[("rb0",)])
        mm(pbc1[:, 0:nc_], onerow[:, :], rb0[:, 0:nc_],
           reads=[("onerow",), ("rb0",)],
           writes=[("pbc1",)], start=True, stop=True)
        for (dst_ap, dst_key, off, w) in dsts:
            tt("vector", dst_ap, qsc[:, off:off + w], pbc1[0:128, off:off + w],
               ALU.mult, reads=[("qsc",), ("pbc1",)], writes=[dst_key])

    def load_xblock(src_d, col0):
        for k in range(16):
            xb_ = xld2_[k % 2]
            dma("sync", xb_[:, :], src_d[c128(k), col0:col0 + 512],
                writes=[("xld", k % 2)])
            cp("scalar", xkb[:, k * 512:(k + 1) * 512], xb_[:, :],
               reads=[("xld", k % 2)], writes=[("xb", k)])

    jcnt = [0]

    def proj_mm(mtile, mw, bi, nc_=512):
        jcnt[0] += 1
        wkey = ("wj", jcnt[0])
        if mw == 128:
            dma("sync", wf32[bi][:, :], wqkv_d[c128(mtile), :],
                writes=[("wf32", bi)])
            cp(rot(), wbf[bi][:, :], wf32[bi][:, :],
               reads=[("wf32", bi)], writes=[("wbf", bi), wkey])
        else:
            srcd = wlr_d if mw == 48 else (wm_d if mtile == -1 else ws_d)
            dma("sync", wf32[bi][:, 0:16 * mw], srcd[:, :],
                writes=[("wf32", bi)])
            cp(rot(), wbf[bi][:, 0:16 * mw], wf32[bi][:, 0:16 * mw],
               reads=[("wf32", bi)], writes=[("wbf", bi), wkey])
        pm = pmA
        for k in range(16):
            mm(pm[0:mw, 0:nc_] if mw < 128 else pm[:, 0:nc_],
               wbf[bi][:, k * mw:(k + 1) * mw],
               xkb[:, k * nc_:k * nc_ + nc_],
               reads=[wkey, ("xb", k)], writes=[("pmA",)],
               start=(k == 0), stop=(k == 15))
        return pm

    def fwd_block(hl, h, blk, wTs):
        tok0 = blk * 512
        rhs = qT[hl][:, tok0:tok0 + 512]
        mm(pmA[:, :], wTs[2][:, :], rhs, reads=[("wT", 2), ("qT", hl)],
           writes=[("pmA",)], start=True, stop=True)
        cp("scalar", s1[:, :], pmA[:, :], reads=[("pmA",)], writes=[("s1",)])
        mm(pmB[:, :], wTs[0][:, :], rhs, reads=[("wT", 0), ("qT", hl)],
           writes=[("pmB",)], start=True, stop=True)
        act(s2[:, :], pmB[:, :], AF.Silu, reads=[("pmB",)], writes=[("s2",)])
        tt("vector", s3[:, :], s2[:, :], s1[:, :], ALU.mult,
           reads=[("s1",), ("s2",)], writes=[("s3",)])
        mm(pmA[:, :], wTs[1][:, :], s3[:, :], reads=[("wT", 1), ("s3",)],
           writes=[("pmA",)], start=True, stop=True)
        act(s4[:, :], pmA[:, :], AF.Copy, reads=[("pmA",)], writes=[("s4",)])
        act(s5[:, :], pmA[:, :], AF.Square, reads=[("pmA",)], writes=[("s5",)])
        mm(pst1[:, :], onecol[:, :], s4[:, :], reads=[("onecol",), ("s4",)],
           writes=[("pst1",)], start=True, stop=True)
        mm(pst2[:, :], onecol[:, :], s5[:, :], reads=[("onecol",), ("s5",)],
           writes=[("pst2",)], start=True, stop=True)
        act(r0[:, :], pst1[:, :], AF.Copy, reads=[("pst1",)], writes=[("r0",)],
            scale=1.0 / 128.0)
        act(r1_[:, :], r0[:, :], AF.Square, reads=[("r0",)], writes=[("r1",)])
        stt(r2_[:, :], pst2[:, :], 1.0 / 128.0, r1_[:, :], ALU.mult, ALU.subtract,
            reads=[("pst2",), ("r1",)], writes=[("r2",)])
        act(r2_[:, :], r2_[:, :], AF.Sqrt, reads=[("r2",), ("cvec",)],
            writes=[("r2",)], bias=cvec[0:1, 1:2])
        recip(r2_[:, :], reads=[("r2",)], writes=[("r2",)])
        mm(pbc1[:, :], selh[0:16, h * 128:(h + 1) * 128],
           S_bf[0:16, tok0:tok0 + 512], reads=[("selh",), ("S_bf",)],
           writes=[("pbc1",)], start=True, stop=True)
        cp("scalar", rb1[:, :], pbc1[0:1, :], reads=[("pbc1",)], writes=[("rb1",)])
        tt("vector", r1_[:, :], r2_[:, :], pbc1[0:1, :], ALU.mult,
           reads=[("r2",), ("pbc1",)], writes=[("r1",)])
        cp("scalar", rb0[:, :], r1_[:, :], reads=[("r1",)], writes=[("rb0",)])
        mm(pbc2[:, :], onerow[:, :], rb0[:, :], reads=[("onerow",), ("rb0",)],
           writes=[("pbc2",)], start=True, stop=True)
        cp("scalar", rb0[:, :], r0[:, :], reads=[("r0",)], writes=[("rb0",)])
        mm(pbc1[:, :], onerow[:, :], rb0[:, :], reads=[("onerow",), ("rb0",)],
           writes=[("pbc1",)], start=True, stop=True)
        tt("vector", qsc[:, :], s4[:, :], pbc1[0:128, :], ALU.subtract,
           reads=[("s4",), ("pbc1",)], writes=[("qsc",)])
        tt("vector", qsc[:, :], qsc[:, :], pbc2[0:128, :], ALU.mult,
           reads=[("qsc",), ("pbc2",)], writes=[("qsc",)])
        tsm("vector", qsc[:, :], qsc[:, :], lng[:, :], reads=[("qsc",), ("lng",)],
            writes=[("qsc",)])
        mm(pbc2[:, :], lnbrow_bf[:, :], rb1[:, :], reads=[("lnbrowb",), ("rb1",)],
           writes=[("pbc2",)], start=True, stop=True)
        tt("vector", A[:, h * 1024 + tok0: h * 1024 + tok0 + 512],
           qsc[:, :], pbc2[0:128, :], ALU.add,
           reads=[("qsc",), ("pbc2",)], writes=[("A", h)])

    es_t = ExitStack()
    sbt = lambda n, sh, dt: es_t.enter_context(nc.sbuf_tensor(n, sh, dt))
    k_nat = sbt("k_nat", [128, 2048], BF16)
    v_nat = sbt("v_nat", [128, 2048], BF16)
    Hd_n = sbt("Hd_n", [128, 2048], BF16)
    DGB_n = sbt("DGB_n", [128, 2048], BF16)
    DHB_n = sbt("DHB_n", [128, 2048], BF16)
    klsc = sbt("klsc", [128, 2048], BF16)
    s1 = sbt("s1", [128, 512], BF16)
    s2 = sbt("s2", [128, 512], BF16)
    s3 = sbt("s3", [128, 512], BF16)
    s4 = sbt("s4", [128, 512], BF16)
    s5 = sbt("s5", [128, 512], BF16)
    s6 = sbt("s6", [128, 512], BF16)
    s7 = sbt("s7", [128, 512], BF16)
    s8 = sbt("s8", [128, 512], BF16)
    sH = sbt("sH", [128, 512], BF16)
    sD = sbt("sD", [128, 512], BF16)
    wN = [sbt("wN%d" % j, [128, 128], F32) for j in range(3)]
    wnorm = [sbt("wno%d" % j, [128, 1], F32) for j in range(3)]
    wT_bf = [sbt("wT%d" % j, [128, 128], BF16) for j in range(3)]
    w1nat_bf = sbt("w1natb", [128, 128], BF16)
    dwst = [sbt("dwst%d" % j, [128, 128], F32) for j in range(3)]
    X_bf = sbt("Xbf", [128, 128], BF16)
    XT_bf = sbt("XTbf", [128, 128], BF16)
    A_bf = sbt("Abf2", [128, 128], BF16)
    Bm_bf = sbt("Bmbf", [128, 128], BF16)
    nst1 = sbt("nst1", [128, 128], F32)
    nsq = sbt("nsq", [128, 128], BF16)
    wsq = sbt("wsq", [128, 128], F32)
    rA = sbt("rA", [1, 1], F32)
    rAb = sbt("rAb", [1, 1], BF16)
    nbv = sbt("nbv", [128, 1], F32)
    rh = sbt("rh", [128, 1], F32)
    wld = sbt("wld", [128, 128], F32)

    for tb in range(4):
        load_xblock(xT_d, tb * 512)
        tok0 = tb * 512
        pm = proj_mm(-2, 48, 0)
        tsa("vector", sc48[0][:, :], pm[0:48, :], BASE_LR_INV,
            reads=[("pmA",)], writes=[("sc0",)])
        act(sc48[1][:, :], sc48[0][:, :], AF.Abs, reads=[("sc0",)],
            writes=[("sc1",)])
        act(sc48[2][:, :], sc48[1][:, :], AF.Exp, reads=[("sc1",)],
            writes=[("sc2",)], scale=-1.0)
        act(sc48[1][:, :], sc48[2][:, :], AF.Ln, reads=[("sc2",)],
            writes=[("sc1",)], bias=1.0)
        act(sc48[2][:, :], sc48[0][:, :], AF.Relu, reads=[("sc0",)],
            writes=[("sc2",)])
        tt("vector", lr_bf[0:48, tok0:tok0 + 512], sc48[1][:, :],
           sc48[2][:, :], ALU.add, reads=[("sc1",), ("sc2",)],
           writes=[("lr_bf",)])
        pm = proj_mm(-1, 16, 1)
        act(M_bf[0:16, tok0:tok0 + 512], pm[0:16, :], AF.Sigmoid,
            reads=[("pmA",), ("bm",)], writes=[("M_bf",)],
            bias=bm[0:16, :])
    for tb2 in range(2):
        load_xblock(xres_d, tb2 * 512)
        tok0 = tb2 * 512
        pm = proj_mm(-3, 16, 0)
        act(S_bf[0:16, tok0:tok0 + 512], pm[0:16, :], AF.Silu,
            reads=[("pmA",), ("bs",)], writes=[("S_bf",)],
            bias=bs[0:16, :])
    rsum(mrow[:, :], M_bf[0:16, :], reads=[("M_bf",)], writes=[("mrow",)])
    p.op("vector", lambda e: e.tensor_scalar(
        mrow[:, :], mrow[:, :], 1.0 / 2048.0, 1.0, ALU.mult, ALU.add),
        reads=[("mrow",)], writes=[("mrow",)])
    cp("scalar", mrow_bf[:, :], mrow[:, :], reads=[("mrow",)],
       writes=[("mrowb",)])

    for grp in range(4):
        for k in range(16):
            xb_ = xld2_[k % 2]
            srcv = xT_d[c128(k), grp * 1024:(grp + 1) * 1024].rearrange(
                "p (i l) -> p i l", l=256)[:, :, 0:128]
            dstv = xb_[:, :].rearrange("p (i l) -> p i l", l=128)
            dma("sync", dstv, srcv, writes=[("xld", k % 2)])
            cp("scalar", xkb[:, k * 512:(k + 1) * 512], xb_[:, :],
               reads=[("xld", k % 2)], writes=[("xb", k)])
        for h16 in range(16):
            bi = h16 % 2
            pm = proj_mm(3 * h16 + 1, 128, bi)
            l2norm_store(pm, 512,
                         [(kT[i][:, h16 * 128:(h16 + 1) * 128], ("kT", i),
                           i * 128, 128) for i in range(4)])
            pm = proj_mm(3 * h16 + 2, 128, bi ^ 1)
            for i in range(4):
                act(vT[i][:, h16 * 128:(h16 + 1) * 128],
                    pm[:, i * 128:(i + 1) * 128], AF.Silu,
                    reads=[("pmA",)], writes=[("vT", i)])
        for k in range(16):
            xb_ = xld2_[k % 2]
            srcv = xq_d[c128(k), :].rearrange("p (h l) -> p h l", h=2)[
                :, :, grp * 128:(grp + 1) * 128]
            dstv = xb_[:, 0:256].rearrange("p (h l) -> p h l", l=128)
            dma("sync", dstv, srcv, writes=[("xld", k % 2)])
            cp("scalar", xkb[:, k * 256:(k + 1) * 256], xb_[:, 0:256],
               reads=[("xld", k % 2)], writes=[("xb", k)])
        for h16 in range(16):
            bi = h16 % 2
            pm = proj_mm(3 * h16, 128, bi, nc_=256)
            l2norm_store(pm, 256, [(s1[:, 0:256], ("s1",), 0, 256)])
            for i in range(4):
                for half in range(2):
                    cp("vector",
                       qT[i][:, half * 512 + h16 * 32:
                             half * 512 + h16 * 32 + 32],
                       s1[:, half * 128 + i * 32: half * 128 + i * 32 + 32],
                       reads=[("s1",)], writes=[("qT", i)])
        p.barrier()
        for hl in range(4):
            h = 4 * grp + hl
            mm(pbc1[:, 0:1], selh[0:16, h * 128:(h + 1) * 128], mrow_bf[:, :],
               reads=[("selh",), ("mrowb",)], writes=[("pbc1",)],
               start=True, stop=True)
            cp("vector", mom[hl][:, :], pbc1[:, 0:1], reads=[("pbc1",)],
               writes=[("mom", hl)])
            for t in range(16):
                mm(pbc2[:, 0:3], lr_bf[0:48, t:2048:16],
                   sellr[0:48, h * 3:h * 3 + 3],
                   reads=[("lr_bf",), ("sellr",)], writes=[("pbc2",)],
                   start=True, stop=True)
                cp("vector", lrc[hl][:, t * 3:t * 3 + 3], pbc2[:, 0:3],
                   reads=[("pbc2",)], writes=[("lrc", hl)])
        for hl in range(4):
            h = 4 * grp + hl
            for j, (nd, td) in enumerate([(w0n_d, w0t_d), (w1n_d, w1t_d),
                                          (w2n_d, w2t_d)]):
                dma("sync", wN[j][:, :], nd[c128(h), :], writes=[("wN", j)])
                dma("sync", wld[:, :], td[c128(h), :], writes=[("wld",)])
                cp("scalar", wT_bf[j][:, :], wld[:, :], reads=[("wld",)],
                   writes=[("wT", j)])
                act(wsq[:, :], wN[j][:, :], AF.Square, reads=[("wN", j)],
                    writes=[("wsq",)])
                rsum(wnorm[j][:, :], wsq[:, :], reads=[("wsq",)],
                     writes=[("wnorm", j)])
                act(wnorm[j][:, :], wnorm[j][:, :], AF.Sqrt,
                    reads=[("wnorm", j)], writes=[("wnorm", j)])
            cp("scalar", w1nat_bf[:, :], wN[1][:, :], reads=[("wN", 1)],
               writes=[("w1natb",)])

            fwd_block(hl, h, 0, wT_bf)
            p.barrier()

            for bi4 in range(4):
                t0 = bi4 * 512
                kb = kT[hl][:, t0:t0 + 512]
                mm(pmA[:, :], wT_bf[0][:, :], kb, reads=[("wT", 0), ("kT", hl)],
                   writes=[("pmA",)], start=True, stop=True)
                act(s1[:, :], pmA[:, :], AF.Sigmoid, reads=[("pmA",)],
                    writes=[("s1",)])
                act(s2[:, :], pmA[:, :], AF.Silu, reads=[("pmA",)],
                    writes=[("s2",)])
                act(s3[:, :], s1[:, :], AF.Copy, reads=[("s1",)], writes=[("s3",)],
                    scale=-1.0, bias=1.0)
                tt("vector", s4[:, :], pmA[:, :], s3[:, :], ALU.mult,
                   reads=[("pmA",), ("s3",)], writes=[("s4",)])
                act(s5[:, :], s4[:, :], AF.Copy, reads=[("s4",)], writes=[("s5",)],
                    bias=1.0)
                mm(pmB[:, :], wT_bf[2][:, :], kb, reads=[("wT", 2), ("kT", hl)],
                   writes=[("pmB",)], start=True, stop=True)
                cp("scalar", s6[:, :], pmB[:, :], reads=[("pmB",)],
                   writes=[("s6",)])
                tt("vector", sH[:, :], s2[:, :], s6[:, :], ALU.mult,
                   reads=[("s2",), ("s6",)], writes=[("sH",)])
                mm(pmA[:, :], w1nat_bf[:, :], vT[hl][:, t0:t0 + 512],
                   reads=[("w1natb",), ("vT", hl)], writes=[("pmA",)],
                   start=True, stop=True)
                cp("scalar", s7[:, :], pmA[:, :], reads=[("pmA",)],
                   writes=[("s7",)])
                tt("vector", sD[:, :], s7[:, :], s2[:, :], ALU.mult,
                   reads=[("s7",), ("s2",)], writes=[("sD",)])
                tt("vector", s8[:, :], s7[:, :], s6[:, :], ALU.mult,
                   reads=[("s7",), ("s6",)], writes=[("s8",)])
                tt("vector", s8[:, :], s8[:, :], s1[:, :], ALU.mult,
                   reads=[("s8",), ("s1",)], writes=[("s8",)])
                tt("vector", s8[:, :], s8[:, :], s5[:, :], ALU.mult,
                   reads=[("s8",), ("s5",)], writes=[("s8",)])
                for (srcten, base, skey, dstn, dkey) in [
                        (sH, 0, ("sH",), Hd_n, ("Hd_n",)),
                        (s8, 0, ("s8",), DGB_n, ("DGB_n",)),
                        (sD, 0, ("sD",), DHB_n, ("DHB_n",)),
                        (kT[hl], t0, ("kT", hl), k_nat, ("k_nat",)),
                        (vT[hl], t0, ("vT", hl), v_nat, ("v_nat",))]:
                    for i in range(4):
                        tr(ptr[:, i * 128:(i + 1) * 128],
                           srcten[:, base + i * 128: base + (i + 1) * 128],
                           reads=[skey, ("idbf",)], writes=[("ptr",)])
                    cp("vector", dstn[:, t0:t0 + 512], ptr[:, :],
                       reads=[("ptr",)], writes=[dkey])

            for j, (lhs_n, lhs_key, rsrc, rkey, lridx, jw) in enumerate([
                    (v_nat, ("v_nat",), Hd_n, ("Hd_n",), 1, 1),
                    (DGB_n, ("DGB_n",), k_nat, ("k_nat",), 0, 0),
                    (DHB_n, ("DHB_n",), k_nat, ("k_nat",), 2, 2)]):
                for t in range(16):
                    tsm(rot2(), klsc[:, c128(t)], rsrc[:, c128(t)],
                        lrc[hl][:, t * 3 + lridx:t * 3 + lridx + 1],
                        reads=[rkey, ("lrc", hl)], writes=[("klsc", t)])
                for t in range(16):
                    mm(pbc1[:, 0:128], lhs_n[:, c128(t)], klsc[:, c128(t)],
                       reads=[lhs_key, ("klsc", t)], writes=[("pbc1",)],
                       start=(t == 0), stop=(t == 15))
                tsm("vector", dwst[j][:, :], pbc1[:, 0:128], mom[hl][:, :],
                    reads=[("pbc1",), ("mom", hl)], writes=[("dw", j)])
                Xcur = dwst[j]
                xk = ("dw", j)
                act(nsq[:, :], Xcur[:, :], AF.Square, reads=[xk],
                    writes=[("nsq",)])
                mm(pst1[:, 0:128], onecol[:, :], nsq[:, :],
                   reads=[("onecol",), ("nsq",)], writes=[("pst1",)],
                   start=True, stop=True)
                rsum(rA[:, :], pst1[0:1, 0:128], reads=[("pst1",)],
                     writes=[("rA",)])
                act(rA[:, :], rA[:, :], AF.Sqrt, reads=[("rA",)], writes=[("rA",)])
                tsa("vector", rA[:, :], rA[:, :], 1e-7, reads=[("rA",)],
                    writes=[("rA",)])
                recip(rA[:, :], reads=[("rA",)], writes=[("rA",)])
                cp("scalar", rAb[:, :], rA[:, :], reads=[("rA",)],
                   writes=[("rAb",)])
                mm(pbc2[:, 0:1], onerow[:, :], rAb[:, :],
                   reads=[("onerow",), ("rAb",)], writes=[("pbc2",)],
                   start=True, stop=True)
                cp("vector", nbv[:, :], pbc2[:, 0:1], reads=[("pbc2",)],
                   writes=[("nbv",)])
                tsm("vector", Xcur[:, :], Xcur[:, :], nbv[:, :],
                    reads=[xk, ("nbv",)], writes=[xk])
                cp("scalar", X_bf[:, :], Xcur[:, :], reads=[xk],
                   writes=[("X_bf",)])
                tr(ptr[:, 0:128], X_bf[:, :], reads=[("X_bf",), ("idbf",)],
                   writes=[("ptr",)])
                cp("vector", XT_bf[:, :], ptr[:, 0:128], reads=[("ptr",)],
                   writes=[("XT_bf",)])
                for it, (ca, cb, cc_) in enumerate(NS_COEFS):
                    mm(pbc1[:, 0:128], XT_bf[:, :], XT_bf[:, :],
                       reads=[("XT_bf",)], writes=[("pbc1",)],
                       start=True, stop=True)
                    cp("scalar", A_bf[:, :], pbc1[:, 0:128], reads=[("pbc1",)],
                       writes=[("A_bf",)])
                    mm(pbc2[:, 0:128], A_bf[:, :], A_bf[:, :], reads=[("A_bf",)],
                       writes=[("pbc2",)], start=True, stop=True)
                    act(nst1[:, :], pbc2[:, 0:128], AF.Copy, reads=[("pbc2",)],
                        writes=[("nst1",)], scale=cc_)
                    stt(Bm_bf[:, :], pbc1[:, 0:128], cb, nst1[:, :], ALU.mult,
                        ALU.add, reads=[("pbc1",), ("nst1",)],
                        writes=[("Bm_bf",)])
                    mm(pbc1[:, 0:128], Bm_bf[:, :], X_bf[:, :],
                       reads=[("Bm_bf",), ("X_bf",)], writes=[("pbc1",)],
                       start=True, stop=True)
                    stt(Xcur[:, :], Xcur[:, :], ca, pbc1[:, 0:128], ALU.mult,
                        ALU.add, reads=[xk, ("pbc1",)], writes=[xk])
                    cp("scalar", X_bf[:, :], Xcur[:, :], reads=[xk],
                       writes=[("X_bf",)])
                    if it < 4:
                        tr(ptr[:, 0:128], X_bf[:, :], reads=[("X_bf",), ("idbf",)],
                           writes=[("ptr",)])
                        cp("vector", XT_bf[:, :], ptr[:, 0:128], reads=[("ptr",)],
                           writes=[("XT_bf",)])
                tt("vector", wN[jw][:, :], wN[jw][:, :], Xcur[:, :], ALU.add,
                   reads=[("wN", jw), xk], writes=[("wN", jw)])
                act(wsq[:, :], wN[jw][:, :], AF.Square, reads=[("wN", jw)],
                    writes=[("wsq",)])
                rsum(rh[:, :], wsq[:, :], reads=[("wsq",)], writes=[("rh",)])
                act(rh[:, :], rh[:, :], AF.Sqrt, reads=[("rh",)], writes=[("rh",)])
                tsa("vector", rh[:, :], rh[:, :], 1e-5, reads=[("rh",)],
                    writes=[("rh",)])
                recip(rh[:, :], reads=[("rh",)], writes=[("rh",)])
                tt("vector", rh[:, :], rh[:, :], wnorm[jw][:, :], ALU.mult,
                   reads=[("rh",), ("wnorm", jw)], writes=[("rh",)])
                tsm("vector", wN[jw][:, :], wN[jw][:, :], rh[:, :],
                    reads=[("wN", jw), ("rh",)], writes=[("wN", jw)])
                cp("scalar", X_bf[:, :], wN[jw][:, :], reads=[("wN", jw)],
                   writes=[("X_bf",)])
                tr(ptr[:, 0:128], X_bf[:, :], reads=[("X_bf",), ("idbf",)],
                   writes=[("ptr",)])
                cp("vector", wT_bf[jw][:, :], ptr[:, 0:128], reads=[("ptr",)],
                   writes=[("wT", jw)])

            p.barrier()
            fwd_block(hl, h, 1, wT_bf)

    es_t.close()
    es1.close()
    p.barrier()

    es2 = ExitStack()
    sb2 = lambda n, sh, dt: es2.enter_context(nc.sbuf_tensor(n, sh, dt))
    rs1 = sb2("rs1", [1, 1024], F32)
    rs2 = sb2("rs2", [1, 1024], F32)
    rt_ = sb2("rt", [1, 1024], F32)
    rmub = sb2("rmub", [1, 1024], BF16)
    rrsb = sb2("rrsb", [1, 1024], BF16)
    xr = [sb2("xr%d" % i, [128, 1024], F32) for i in range(2)]
    x1p = sb2("x1p", [128, 1024], F32)
    suug = sb2("suug_s", [128, 16], F32)
    sudp = sb2("sudp_s", [128, 64], F32)
    wsug = sb2("wsug_s", [128, 128], F32)
    wsdp = sb2("wsdp_s", [128, 16], F32)
    tmpb = sb2("tmpb", [128, 4096], BF16)
    tscr = sb2("tscr", [128, 512], F32)
    tscr2 = sb2("tscr2", [128, 512], F32)
    sgb = sb2("sgb", [128, 512], BF16)
    upb = sb2("upb", [128, 512], BF16)
    x2t = sb2("x2t", [128, 512], F32)
    x2b = sb2("x2b", [128, 512], BF16)
    x2q = sb2("x2q", [128, 512], BF16)

    dma("sync", suug[:, :], suug_d[:, :], writes=[("suug",)])
    dma("sync", sudp[:, :], sudp_d[:, :], writes=[("sudp",)])
    dma("sync", wsug[:, :], wsug_d[:, :], writes=[("wsug",)])
    dma("sync", wsdp[:, :], wsdp_d[:, :], writes=[("wsdp",)])

    es2a = ExitStack()
    B = es2a.enter_context(nc.sbuf_tensor("Bbuf", [128, 16384], BF16))
    C = es2a.enter_context(nc.sbuf_tensor("Cbuf", [128, 16384], BF16))
    xpb = es2a.enter_context(nc.sbuf_tensor("xpb", [128, 1024], BF16))
    xqb = es2a.enter_context(nc.sbuf_tensor("xqb", [128, 1024], BF16))
    p.op("vector", lambda e: e.memset(rs1[:, :], 0.0), writes=[("rs1",)])
    p.op("vector", lambda e: e.memset(rs2[:, :], 0.0), writes=[("rs2",)])

    for m in range(16):
        bi = m % 2
        dma("sync", wf32[bi][:, :], wo_d[c128(m), :], writes=[("wf32", bi)])
        cp(rot(), wbf[bi][:, :], wf32[bi][:, :], reads=[("wf32", bi)],
           writes=[("wbf", bi)])
        dma("gpsimd", xr[bi][:, :], xres_d[c128(m), :], writes=[("xr", bi)])
        for half in range(2):
            for k in range(16):
                mm(pmA[:, :], wbf[bi][:, c128(k)],
                   A[:, k * 1024 + half * 512: k * 1024 + half * 512 + 512],
                   reads=[("wbf", bi), ("A", k)], writes=[("pmA",)],
                   start=(k == 0), stop=(k == 15))
            tt("vector", x1p[:, half * 512:(half + 1) * 512], pmA[:, :],
               xr[bi][:, half * 512:(half + 1) * 512], ALU.add,
               reads=[("pmA",), ("xr", bi)], writes=[("x1p",)])
        cp("scalar", xpb[:, :], x1p[:, :], reads=[("x1p",)], writes=[("xpb",)])
        act(xqb[:, :], x1p[:, :], AF.Square, reads=[("x1p",)], writes=[("xqb",)])
        dma("gpsimd", x1pd[c128(m), :], x1p[:, :], reads=[("x1p",)],
            writes=[("x1pd", m)])
        for half in range(2):
            hs = slice(half * 512, (half + 1) * 512)
            mm(pst1[:, :], onecol[:, :], xpb[:, hs], reads=[("onecol",), ("xpb",)],
               writes=[("pst1",)], start=True, stop=True)
            tt("vector", rs1[:, hs], rs1[:, hs], pst1[:, :], ALU.add,
               reads=[("rs1",), ("pst1",)], writes=[("rs1",)])
            mm(pst2[:, :], onecol[:, :], xqb[:, hs], reads=[("onecol",), ("xqb",)],
               writes=[("pst2",)], start=True, stop=True)
            tt("vector", rs2[:, hs], rs2[:, hs], pst2[:, :], ALU.add,
               reads=[("rs2",), ("pst2",)], writes=[("rs2",)])

    def ln_finalize(eps):
        tsm("vector", rs1[:, :], rs1[:, :], 1.0 / 2048.0,
            reads=[("rs1",)], writes=[("rs1",)])
        act(rt_[:, :], rs1[:, :], AF.Square, reads=[("rs1",)], writes=[("rt",)])
        stt(rs2[:, :], rs2[:, :], 1.0 / 2048.0, rt_[:, :], ALU.mult, ALU.subtract,
            reads=[("rs2",), ("rt",)], writes=[("rs2",)])
        act(rs2[:, :], rs2[:, :], AF.Sqrt, reads=[("rs2",), ("cvec",)],
            writes=[("rs2",)], bias=cvec[0:1, 2:3])
        recip(rs2[:, :], reads=[("rs2",)], writes=[("rs2",)])
        cp("scalar", rmub[:, :], rs1[:, :], reads=[("rs1",)], writes=[("rmub",)])
        cp("scalar", rrsb[:, :], rs2[:, :], reads=[("rs2",)], writes=[("rrsb",)])

    ln_finalize(1e-5)

    for m in range(16):
        bi = m % 2
        dma("sync", xr[bi][:, :], x1pd[c128(m), :], reads=[("x1pd", m)],
            writes=[("xr", bi)])
        for half in range(2):
            hs = slice(half * 512, (half + 1) * 512)
            mm(pbc1[:, :], onerow[:, :], rmub[:, hs], reads=[("onerow",), ("rmub",)],
               writes=[("pbc1",)], start=True, stop=True)
            mm(pbc2[:, :], onerow[:, :], rrsb[:, hs], reads=[("onerow",), ("rrsb",)],
               writes=[("pbc2",)], start=True, stop=True)
            tt("vector", x1p[:, hs], xr[bi][:, hs], pbc1[0:128, :], ALU.subtract,
               reads=[("xr", bi), ("pbc1",)], writes=[("x1p",)])
            tt("vector", x1p[:, hs], x1p[:, hs], pbc2[0:128, :], ALU.mult,
               reads=[("x1p",), ("pbc2",)], writes=[("x1p",)])
        cp("scalar", B[:, m * 1024:(m + 1) * 1024], x1p[:, :],
           reads=[("x1p",)], writes=[("B", m)])
        dma("gpsimd", x1d[c128(m), :], x1p[:, :], reads=[("x1p",)],
            writes=[("x1d", m)])

    for t in range(16):
        tsm(rot2(), C[:, t * 1024:(t + 1) * 1024], B[:, t * 1024:(t + 1) * 1024],
            suug[:, t:t + 1], reads=[("B", t), ("suug",)], writes=[("C", t)])

    def butterfly(buf, bufkey, ntiles, width, stage_d):
        for pr in range(ntiles // (2 * stage_d)):
            a0 = (2 * pr) * stage_d
            b0 = (2 * pr + 1) * stage_d
            blockw = stage_d * width
            off = 0
            while off < blockw:
                w = min(4096, blockw - off)
                aslc = slice(a0 * width + off, a0 * width + off + w)
                bslc = slice(b0 * width + off, b0 * width + off + w)
                akeys = [(bufkey, a0 + (off + q) // width)
                         for q in range(0, w, width)]
                bkeys = [(bufkey, b0 + (off + q) // width)
                         for q in range(0, w, width)]
                cp("gpsimd", tmpb[:, 0:w], buf[:, aslc],
                   reads=akeys, writes=[("tmpb",)])
                tt("vector", buf[:, aslc], tmpb[:, 0:w], buf[:, bslc], ALU.add,
                   reads=[("tmpb",)] + bkeys, writes=akeys)
                tt("gpsimd", buf[:, bslc], tmpb[:, 0:w], buf[:, bslc],
                   ALU.subtract, reads=[("tmpb",)] + bkeys, writes=bkeys)
                off += w

    for d in (1, 2, 4, 8):
        butterfly(C, "C", 16, 1024, d)

    p.barrier()
    for t in range(16):
        for half in range(2):
            hs = slice(t * 1024 + half * 512, t * 1024 + half * 512 + 512)
            mm(pbc1[:, :], h128[:, :], C[:, hs], reads=[("h128",), ("C", t)],
               writes=[("pbc1",)], start=True, stop=True)
            cp("scalar" if half == 0 else "vector", A[:, hs], pbc1[:, :],
               reads=[("pbc1",)], writes=[("A2", t)])

    es2a.close()
    p.barrier()

    es2b = ExitStack()
    g = es2b.enter_context(nc.sbuf_tensor("gbuf", [128, 32768], BF16))
    wdb = es2b.enter_context(nc.sbuf_tensor("wdb", [128, 8192], BF16))

    p.op("vector", lambda e: e.memset(rs1[:, :], 0.0), writes=[("rs1",)])
    p.op("vector", lambda e: e.memset(rs2[:, :], 0.0), writes=[("rs2",)])

    for th in range(2):
        hoff = th * 512
        for m in range(64):
            biu = m % 2
            big = (m + 1) % 2
            dma("sync", wf32[biu][:, :], wu_d[c128(m), :],
                writes=[("wf32", biu)])
            cp(rot(), wbf[biu][:, :], wf32[biu][:, :], reads=[("wf32", biu)],
               writes=[("wbf", biu)])
            for k in range(16):
                mm(pmA[:, :], wbf[biu][:, c128(k)],
                   A[:, k * 1024 + hoff: k * 1024 + hoff + 512],
                   reads=[("wbf", biu), ("A2", k)], writes=[("pmA",)],
                   start=(k == 0), stop=(k == 15))
            tsm("vector", tscr[:, :], pmA[:, :], wsug[:, m:m + 1],
                reads=[("pmA",), ("wsug",)], writes=[("tscr",)])
            act(upb[:, :], tscr[:, :], AF.Copy, reads=[("tscr",)],
                writes=[("upb",)], scale=RS2048)
            dma("scalar", wf32[big][:, :], wg_d[c128(m), :],
                writes=[("wf32", big)])
            cp(rot(), wbf[big][:, :], wf32[big][:, :], reads=[("wf32", big)],
               writes=[("wbf", big)])
            for k in range(16):
                mm(pmB[:, :], wbf[big][:, c128(k)],
                   A[:, k * 1024 + hoff: k * 1024 + hoff + 512],
                   reads=[("wbf", big), ("A2", k)], writes=[("pmB",)],
                   start=(k == 0), stop=(k == 15))
            tsm("vector", tscr2[:, :], pmB[:, :], wsug[:, 64 + m:65 + m],
                reads=[("pmB",), ("wsug",)], writes=[("tscr2",)])
            act(sgb[:, :], tscr2[:, :], AF.Silu, reads=[("tscr2",)],
                writes=[("sgb",)], scale=RS2048)
            tt("vector", tscr[:, :], sgb[:, :], upb[:, :], ALU.mult,
               reads=[("sgb",), ("upb",)], writes=[("tscr",)])
            tsm("gpsimd", g[:, m * 512:(m + 1) * 512], tscr[:, :],
                sudp[:, m:m + 1], reads=[("tscr",), ("sudp",)],
                writes=[("G", m)])

        for d in (1, 2, 4, 8, 16, 32):
            butterfly(g, "G", 64, 512, d)
        for t in range(64):
            gs = slice(t * 512, (t + 1) * 512)
            mm(pbc1[:, :], h128[:, :], g[:, gs], reads=[("h128",), ("G", t)],
               writes=[("pbc1",)], start=True, stop=True)
            cp("scalar" if t % 2 == 0 else "vector", g[:, gs], pbc1[:, :],
               reads=[("pbc1",)], writes=[("G", t)])

        for m in range(16):
            bi = m % 2
            for ch in range(4):
                dma("sync" if ch % 2 == 0 else "scalar", wf32[bi][:, :],
                    wd_d[c128(m), ch * 2048:(ch + 1) * 2048],
                    writes=[("wf32", bi)])
                cp(rot(), wdb[:, ch * 2048:(ch + 1) * 2048], wf32[bi][:, :],
                   reads=[("wf32", bi)], writes=[("wdb", ch)])
                for kk in range(16):
                    k = ch * 16 + kk
                    mm(pmA[:, :], wdb[:, c128(k)], g[:, k * 512:(k + 1) * 512],
                       reads=[("wdb", ch), ("G", k)], writes=[("pmA",)],
                       start=(k == 0), stop=(k == 63))
            tsm("vector", tscr[:, :], pmA[:, :], wsdp[:, m:m + 1],
                reads=[("pmA",), ("wsdp",)], writes=[("tscr",)])
            dma("gpsimd", xr[0][:, :], x1d[c128(m), :], reads=[("x1d", m)],
                writes=[("xr", 0)])
            stt(x2t[:, :], tscr[:, :], RS8192, xr[0][:, hoff:hoff + 512],
                ALU.mult, ALU.add, reads=[("tscr",), ("xr", 0)],
                writes=[("x2t",)])
            cp("scalar", x2b[:, :], x2t[:, :], reads=[("x2t",)], writes=[("x2b",)])
            act(x2q[:, :], x2t[:, :], AF.Square, reads=[("x2t",)],
                writes=[("x2q",)])
            dma("gpsimd", x2d[c128(m), hoff:hoff + 512], x2t[:, :],
                reads=[("x2t",)], writes=[("x2d", m)])
            hs = slice(hoff, hoff + 512)
            mm(pst1[:, :], onecol[:, :], x2b[:, :], reads=[("onecol",), ("x2b",)],
               writes=[("pst1",)], start=True, stop=True)
            tt("vector", rs1[:, hs], rs1[:, hs], pst1[:, :], ALU.add,
               reads=[("rs1",), ("pst1",)], writes=[("rs1",)])
            mm(pst2[:, :], onecol[:, :], x2q[:, :], reads=[("onecol",), ("x2q",)],
               writes=[("pst2",)], start=True, stop=True)
            tt("vector", rs2[:, hs], rs2[:, hs], pst2[:, :], ALU.add,
               reads=[("rs2",), ("pst2",)], writes=[("rs2",)])

    ln_finalize(1e-5)
    for m in range(16):
        bi = m % 2
        dma("sync", xr[bi][:, :], x2d[c128(m), :], reads=[("x2d", m)],
            writes=[("xr", bi)])
        for half in range(2):
            hs = slice(half * 512, (half + 1) * 512)
            mm(pbc1[:, :], onerow[:, :], rmub[:, hs], reads=[("onerow",), ("rmub",)],
               writes=[("pbc1",)], start=True, stop=True)
            mm(pbc2[:, :], onerow[:, :], rrsb[:, hs], reads=[("onerow",), ("rrsb",)],
               writes=[("pbc2",)], start=True, stop=True)
            tt("vector", x1p[:, hs], xr[bi][:, hs], pbc1[0:128, :], ALU.subtract,
               reads=[("xr", bi), ("pbc1",)], writes=[("x1p",)])
            tt("vector", x1p[:, hs], x1p[:, hs], pbc2[0:128, :], ALU.mult,
               reads=[("x1p",), ("pbc2",)], writes=[("x1p",)])
        dma("gpsimd", outT_d[c128(m), :], x1p[:, :], reads=[("x1p",)],
            writes=[("outd", m)])

    p.barrier()
    es2b.close()
    es2.close()

    with nc.Block() as block:
        def mk(engname):
            def section(eng, _items=p.items[engname], _n=engname):
                for waits, fn, inc in _items:
                    for (src, val) in waits:
                        eng.wait_ge(sems[src], val)
                    if fn is not None:
                        instr = fn(eng)
                        if inc:
                            instr.then_inc(sems[_n], inc)
            return section

        block.tensor(mk("tensor"))
        block.vector(mk("vector"))
        block.scalar(mk("scalar"))
        block.gpsimd(mk("gpsimd"))
        block.sync(mk("sync"))

    es.close()
    return nc


def _blk(w):
    K, M = w.shape
    return np.ascontiguousarray(
        w.reshape(K // 128, 128, M // 128, 128).transpose(2, 1, 0, 3)
        .reshape(M, K))


def _hadamard128():
    H = np.array([[1.0]], dtype=np.float64)
    for _ in range(7):
        H = np.block([[H, H], [H, -H]])
    return H.astype(ml_dtypes.bfloat16)


def kernel(**inp):
    bf = ml_dtypes.bfloat16
    f32 = np.float32
    x = np.asarray(inp["L0_attn_x"], f32)
    Wqkv = np.asarray(inp["L0_attn_W_qkv"], f32)
    Wlr = np.asarray(inp["L0_attn_W_lr"], f32)
    Wo = np.asarray(inp["L0_attn_W_o"], f32)
    Wm = np.asarray(inp["L0_attn_W_m"], f32)
    b_m = np.asarray(inp["L0_attn_b_m"], f32)
    Ws = np.asarray(inp["L0_attn_W_s"], f32)
    b_s = np.asarray(inp["L0_attn_b_s"], f32)
    ln_g = np.asarray(inp["L0_attn_ln_g"], f32)
    ln_b = np.asarray(inp["L0_attn_ln_b"], f32)
    w0 = np.asarray(inp["L0_attn_w0"], f32)
    w1 = np.asarray(inp["L0_attn_w1"], f32)
    w2 = np.asarray(inp["L0_attn_w2"], f32)
    SU_ug = np.asarray(inp["L0_ffn_SU_ug"], f32)
    SU_dp = np.asarray(inp["L0_ffn_SU_dp"], f32)
    Wsc_ug = np.asarray(inp["L0_ffn_Wscale_ug"], f32)
    Wsc_dp = np.asarray(inp["L0_ffn_Wscale_dp"], f32)
    W_up = np.asarray(inp["L0_ffn_W_up"], f32)
    W_gate = np.asarray(inp["L0_ffn_W_gate"], f32)
    W_down = np.asarray(inp["L0_ffn_W_down"], f32)

    nc = build_nc()

    h128 = np.ascontiguousarray(_hadamard128())
    idbf = np.eye(128, dtype=bf)
    onecol = np.ones((128, 1), dtype=bf)
    onerow = np.ones((1, 128), dtype=bf)

    xT = [np.ascontiguousarray(x[b].T) for b in range(2)]
    wu_b = _blk(np.ascontiguousarray(W_up.T))
    wg_b = _blk(np.ascontiguousarray(W_gate.T))
    wd_b = _blk(np.ascontiguousarray(W_down.T))
    wlr_b = np.ascontiguousarray(
        Wlr.reshape(16, 128, 48).transpose(1, 0, 2).reshape(128, 768))
    wm_b = np.ascontiguousarray(
        Wm.reshape(16, 128, 16).transpose(1, 0, 2).reshape(128, 256))
    ws_b = np.ascontiguousarray(
        Ws.reshape(16, 128, 16).transpose(1, 0, 2).reshape(128, 256))

    in_maps = []
    wqkv_b = _blk(Wqkv)
    wo_full = _blk(Wo)
    selh = np.zeros((16, 2048), dtype=bf)
    for h in range(16):
        selh[h, h * 128:(h + 1) * 128] = 1
    sellr = np.zeros((48, 48), dtype=bf)
    for h in range(16):
        for j in range(3):
            sellr[j * 16 + h, h * 3 + j] = 1
    w_com = {
        "wqkv": wqkv_b, "wlr": wlr_b, "wm": wm_b, "ws": ws_b,
        "wo": wo_full, "wu": wu_b, "wg": wg_b, "wd": wd_b,
        "w0n": np.ascontiguousarray(w0.reshape(2048, 128)),
        "w1n": np.ascontiguousarray(w1.reshape(2048, 128)),
        "w2n": np.ascontiguousarray(w2.reshape(2048, 128)),
        "w0t": np.ascontiguousarray(w0.transpose(0, 2, 1).reshape(2048, 128)),
        "w1t": np.ascontiguousarray(w1.transpose(0, 2, 1).reshape(2048, 128)),
        "w2t": np.ascontiguousarray(w2.transpose(0, 2, 1).reshape(2048, 128)),
        "lng": np.ascontiguousarray(ln_g.reshape(128, 1)),
        "lnb": np.ascontiguousarray(ln_b.reshape(1, 128)),
        "bm": np.ascontiguousarray(b_m.reshape(16, 1)),
        "bs": np.ascontiguousarray(b_s.reshape(16, 1)),
        "suug": np.ascontiguousarray(SU_ug.reshape(16, 128).T),
        "sudp": np.ascontiguousarray(SU_dp.reshape(64, 128).T),
        "wsug": np.ascontiguousarray(Wsc_ug.reshape(128, 128).T),
        "wsdp": np.ascontiguousarray(Wsc_dp.reshape(16, 128).T),
        "idbf": idbf, "h128": h128, "onecol": onecol, "onerow": onerow,
        "cvec": np.tile(np.array([[BASE_LR_INV, 1e-6, 1e-5]], np.float32),
                        (128, 1)),
        "selh": selh, "sellr": sellr,
    }
    off = np.arange(32)
    for c in range(8):
        beta, gidx = c // 4, c % 4
        iq = np.concatenate([t * 256 + 128 * half + 32 * gidx + off
                             for half in (0, 1) for t in range(16)])
        ires = np.concatenate([2048 * half + 512 * gidx + off * 16 + h
                               for half in (0, 1) for h in range(16)])
        m = dict(w_com)
        m["xT"] = xT[beta]
        m["xq"] = np.ascontiguousarray(xT[beta][:, iq])
        m["xres"] = np.ascontiguousarray(xT[beta][:, ires])
        m["_ires"] = ires
        in_maps.append(m)

    dev_maps = [{k: v for k, v in m.items() if not k.startswith("_")}
                for m in in_maps]
    res = run_bass_kernel_spmd(nc, dev_maps, core_ids=list(range(8)))

    out = np.empty((2, 4096, 2048), np.float32)
    for c in range(8):
        beta = c // 4
        oT = np.asarray(res.results[c]["outT"])
        out[beta, in_maps[c]["_ires"], :] = oT.T
    return out


if __name__ == "__main__":
    nc = build_nc()
    print("build OK")
